# version 11
# baseline (speedup 1.0000x reference)
import math
import numpy as np

import concourse.bass as bass
import concourse.mybir as mybir
from concourse.bass_utils import run_bass_kernel_spmd

F32 = mybir.dt.float32
BF16 = mybir.dt.bfloat16
AF = mybir.ActivationFunctionType
ALU = mybir.AluOpType
ET = mybir.EngineType

B, S, D = 2, 4096, 4096
NHEAD, HD = 32, 128
GROUP = 4
TOK = S // GROUP
NCH = D // 128
NSLOT = 8
SCALE = 1.0 / math.sqrt(HD)
FH, FDH, FM = 16, 256, 1024
EPS = 1e-5
NEG = -1.0e9

_PHASES = 5
_P2QK = 64
_P2V = True
_DUMPQKV = False
_DUMPO = False

W_SLOT = [64, 128, 256, 512, 512, 512, 512, 512]
SLOPE_MIN = [2.0 ** (-(j + 1)) for j in range(8)]
SKIP_MARGIN = 95.0


def _slopes():
    m = (np.arange(1, 33) * (8.0 / 32)).astype(np.float64)
    return 1.0 / (2.0 ** m)


def attn_schedule():
    sched = []
    for j in range(NSLOT):
        W = W_SLOT[j]
        smin = SLOPE_MIN[j]
        per_qc = []
        for qc in range(8):
            items = []
            for kt in range(4 * qc + 4):
                qlo_c = max(0, 128 * kt - 512 * qc)
                sl = []
                a0 = (qlo_c // W) * W
                for a in range(a0, 512, W):
                    b = a + W
                    aa = max(a, qlo_c)
                    if aa >= b:
                        continue
                    dist = (512 * qc + aa) - (128 * kt + 127)
                    if dist * smin > SKIP_MARGIN:
                        continue
                    qref = 512 * qc + b - 1
                    sl.append((aa, b, 128 * kt - qref))
                if sl:
                    items.append((kt, sl[0][0], sl))
            per_qc.append(items)
        sched.append(per_qc)
    return sched


def bias_keys(sched):
    keys = []
    seen = set()
    for j in range(NSLOT):
        for qc in range(8):
            for _, _, sl in sched[j][qc]:
                for _, _, d in sl:
                    if (j, d) not in seen:
                        seen.add((j, d))
                        keys.append((j, d))
    return {k: i for i, k in enumerate(keys)}



ENGS = ("PE", "ACT", "DVE", "GP", "SY")
DEF_SEM = {"PE": "PE", "ACT": "ACT", "DVE": "DVE", "GP": "DW", "SY": "DS"}


class Em:
    def __init__(self, sems):
        self.sems = sems
        self.c = {k: 0 for k in sems}
        self.mil = {}
        self.engine = "SIM"
        self.eng = None
        self.last_wait = {}

    def snap(self):
        return dict(self.c)

    def restore(self, snap):
        self.c = dict(snap)
        self.last_wait = {}

    def wait(self, *names):
        for n in names:
            semname, val = self.mil[n]
            if self.eng is None:
                continue
            if DEF_SEM.get(self.engine) == semname:
                continue
            if self.last_wait.get(semname, -1) >= val:
                continue
            self.last_wait[semname] = val
            self.eng.wait_ge(self.sems[semname], val)

    def wait_own(self, name):
        semname, val = self.mil[name]
        if self.eng is not None:
            if self.last_wait.get(semname, -1) < val:
                self.last_wait[semname] = val
                self.eng.wait_ge(self.sems[semname], val)

    def step(self, engine, fn, name=None, waits=(), inc=None, sem=None,
             own_waits=()):
        sem = sem or DEF_SEM[engine]
        inc = inc if inc is not None else (16 if sem in ("DW", "DS", "DA")
                                           else 1)
        self.c[sem] += inc
        if name is not None:
            prev = self.mil.get(name)
            assert prev is None or prev == (sem, self.c[sem]), (name, prev)
            self.mil[name] = (sem, self.c[sem])
        if self.engine != engine:
            return
        self.wait(*waits)
        for w in own_waits:
            self.wait_own(w)
        last = fn(self.eng)
        if last is not None:
            last.then_inc(self.sems[sem], inc)

    def raw(self, engine, fn, name=None, waits=(), inc=0, sem=None):
        sem = sem or DEF_SEM[engine]
        self.c[sem] += inc
        if name is not None:
            self.mil[name] = (sem, self.c[sem])
        if self.engine != engine:
            return
        self.wait(*waits)
        fn(self.eng, self.sems[sem])



def build_program(nbk):
    sched = attn_schedule()
    bkeys = bias_keys(sched)
    assert nbk == len(bkeys)

    nc = bass.Bass("TRN2", target_bir_lowering=False)

    x_d = nc.dram_tensor("x_in", [D, TOK], F32, kind="ExternalInput")
    w1qk_d = nc.dram_tensor("w1qk", [16 * 128, 4096], F32, kind="ExternalInput")
    w1v_d = nc.dram_tensor("w1v", [4 * 128, 8192], F32, kind="ExternalInput")
    w2_d = nc.dram_tensor("w2", [32 * 128, 4096], F32, kind="ExternalInput")
    up_d = nc.dram_tensor("up_w", [16 * 128, 2048], F32, kind="ExternalInput")
    dn_d = nc.dram_tensor("dn_w", [16 * 128, 2048], F32, kind="ExternalInput")
    bqk_d = nc.dram_tensor("bqk", [128, 16], F32, kind="ExternalInput")
    bv_d = nc.dram_tensor("bvb", [128, 1024], F32, kind="ExternalInput")
    bout_d = nc.dram_tensor("bout", [128, 32], F32, kind="ExternalInput")
    btab_d = nc.dram_tensor("btab", [128, nbk], F32, kind="ExternalInput")
    etab_d = nc.dram_tensor("etab", [128, 8 * 512], F32, kind="ExternalInput")
    mask_d = nc.dram_tensor("maskt", [128, 128], F32, kind="ExternalInput")
    onesc_d = nc.dram_tensor("onesc", [128, 128], F32, kind="ExternalInput")
    onecol_d = nc.dram_tensor("onecol", [128, 1], F32, kind="ExternalInput")
    ones1_d = nc.dram_tensor("onesr", [1, 128], F32, kind="ExternalInput")
    y_d = nc.dram_tensor("y", [D, TOK], F32, kind="ExternalOutput")
    if _DUMPQKV:
        dbgq_d = nc.dram_tensor("dbgq", [16 * 128, S], BF16,
                                kind="ExternalOutput")
        dbgv_d = nc.dram_tensor("dbgv", [S, 1024], BF16,
                                kind="ExternalOutput")
    if _DUMPO:
        dbgo_d = nc.dram_tensor("dbgo", [NSLOT * 128, S], BF16,
                                kind="ExternalOutput")

    xln_bounce = nc.dram_tensor("xln_bounce", [D, TOK], BF16)
    xln_gather = nc.dram_tensor("xln_gather", [GROUP * D, TOK], BF16)
    qk_spill = nc.dram_tensor("qk_spill", [16 * 128, S], BF16)
    v_spill = nc.dram_tensor("v_spill", [S, 1024], BF16)
    o_bounce = nc.dram_tensor("o_bounce", [NSLOT * 128, S], BF16)
    o_gather = nc.dram_tensor("o_gather", [GROUP * NSLOT * 128, S], BF16)
    r_spill = nc.dram_tensor("r_spill", [D, TOK], BF16)
    upbf_d = nc.dram_tensor("upbf", [16 * 128, 2048], BF16)
    dnbf_d = nc.dram_tensor("dnbf", [16 * 128, 2048], BF16)
    wqbf_d = nc.dram_tensor("wqbf", [16 * 128, 4096], BF16)
    wvbf_d = nc.dram_tensor("wvbf", [4 * 128, 8192], BF16)

    sem_names = ("PE", "ACT", "DVE", "DW", "DS", "DA", "CC1", "CC2")

    from contextlib import ExitStack
    stack = ExitStack()
    sems = {n: stack.enter_context(nc.semaphore("s" + n)) for n in sem_names}
    em = Em(sems)

    Rh = [None]
    mask_sb = stack.enter_context(nc.sbuf_tensor("mask_sb", [128, 128], F32))
    ones_bf = stack.enter_context(nc.sbuf_tensor("ones_bf", [128, 128], BF16))
    onec_bf = stack.enter_context(nc.sbuf_tensor("onec_bf", [128, 1], BF16))
    ones1_sb = stack.enter_context(nc.sbuf_tensor("ones1_sb", [1, 128], F32))
    bqk_sb = stack.enter_context(nc.sbuf_tensor("bqk_sb", [128, 16], F32))
    bout_sb = stack.enter_context(nc.sbuf_tensor("bout_sb", [128, 32], F32))
    btab_sb = stack.enter_context(nc.sbuf_tensor("btab_sb", [128, nbk], F32))

    def run_phase(name, fn):
        snap = em.snap()
        em.engine, em.eng = "SIM", None
        fn(em)
        with nc.Block(name) as blk:
            for engname, reg in (("PE", blk.tensor), ("ACT", blk.scalar),
                                 ("DVE", blk.vector), ("GP", blk.gpsimd),
                                 ("SY", blk.sync)):
                def body(eng, engname=engname):
                    em.restore(snap)
                    em.engine, em.eng = engname, eng
                    fn(em)
                reg(body)
        em.restore(snap)
        em.engine, em.eng = "SIM", None
        fn(em)

    def p0(em):
        xap = x_d[:, :].rearrange("(c p) t -> p c t", p=128)
        rap = Rh[0][:, :].rearrange("p (c t) -> p c t", c=NCH)
        for q in range(8):
            em.step("GP", lambda g, q=q: g.dma_start(
                out=rap[:, q * 4:(q + 1) * 4, :],
                in_=xap[:, q * 4:(q + 1) * 4, :]),
                name=f"ld_x{q}")
        em.step("SY", lambda s: s.dma_start(out=mask_sb[:, :], in_=mask_d[:, :]))
        em.step("GP", lambda g: g.dma_start(out=ones_bf[:, :], in_=onesc_d[:, :]))
        em.step("GP", lambda g: g.dma_start(out=onec_bf[:, :], in_=onecol_d[:, :]))
        em.step("SY", lambda s: s.dma_start(out=ones1_sb[:, :], in_=ones1_d[:, :]))
        em.step("SY", lambda s: s.dma_start(out=bqk_sb[:, :], in_=bqk_d[:, :]))
        em.step("SY", lambda s: s.dma_start(out=bout_sb[:, :], in_=bout_d[:, :]))
        em.step("SY", lambda s: s.dma_start(out=btab_sb[:, :], in_=btab_d[:, :]),
                name="ld_consts")
        em.step("GP", lambda g: g.wait_ge(em.sems["DW"], em.c["DW"]) and None,
                inc=0)
        em.step("SY", lambda s: s.wait_ge(em.sems["DS"], em.c["DS"]) and None,
                inc=0)
        em.step("ACT", lambda a: a.wait_ge(em.sems["DA"], em.c["DA"]) and None,
                inc=0)

    def ln_emit(em, tag, t, apply_only_stats=True):
        (sq_sb, row_mean, row_ms, row_rstd, row_mrs,
         sum_ps, sq_ps, bc_ps, rstdb_sb, mrsb_sb) = t
        for c in range(NCH):
            em.step("DVE", lambda v, c=c: v.tensor_tensor(
                sq_sb[:, (c % 2) * TOK:(c % 2 + 1) * TOK],
                Rh[0][:, c * TOK:(c + 1) * TOK],
                Rh[0][:, c * TOK:(c + 1) * TOK], ALU.mult),
                name=f"{tag}_sq{c}",
                waits=(f"{tag}_st{c-2}",) if c >= 2 else ())
            def mm(tt, c=c):
                for h in range(2):
                    tt.matmul(sum_ps[0:1, h * 512:(h + 1) * 512],
                              onec_bf[:, :],
                              Rh[0][:, c * TOK + h * 512: c * TOK + (h + 1) * 512],
                              start=(c == 0), stop=(c == NCH - 1))
                last = None
                for h in range(2):
                    last = tt.matmul(
                        sq_ps[0:1, h * 512:(h + 1) * 512], onec_bf[:, :],
                        sq_sb[:, (c % 2) * TOK + h * 512:
                              (c % 2) * TOK + (h + 1) * 512],
                        start=(c == 0), stop=(c == NCH - 1))
                return last
            em.step("PE", mm, name=f"{tag}_st{c}", waits=(f"{tag}_sq{c}",))
        em.step("DVE", lambda v: v.tensor_scalar_mul(
            row_mean[:, :], sum_ps[0:1, :], 1.0 / D),
            name=f"{tag}_mean", waits=(f"{tag}_st{NCH-1}",))
        em.step("DVE", lambda v: v.tensor_scalar_mul(
            row_ms[:, :], sq_ps[0:1, :], 1.0 / D), name=f"{tag}_ms")
        em.step("ACT", lambda a: a.activation(
            row_rstd[:, :], row_mean[:, :], AF.Square),
            name=f"{tag}_m2", waits=(f"{tag}_mean",))
        em.step("DVE", lambda v: v.tensor_tensor(
            row_ms[:, :], row_ms[:, :], row_rstd[:, :], ALU.subtract),
            name=f"{tag}_var0", waits=(f"{tag}_m2",), own_waits=(f"{tag}_ms",))
        em.step("DVE", lambda v: v.tensor_scalar_add(
            row_ms[:, :], row_ms[:, :], EPS),
            name=f"{tag}_var", own_waits=(f"{tag}_var0",))
        em.step("ACT", lambda a: a.activation(
            row_rstd[:, :], row_ms[:, :], AF.Ln),
            name=f"{tag}_ln", waits=(f"{tag}_var",))
        em.step("ACT", lambda a: a.activation(
            row_rstd[:, :], row_rstd[:, :], AF.Exp, scale=-0.5),
            name=f"{tag}_rstd", own_waits=(f"{tag}_ln",))
        em.step("DVE", lambda v: v.tensor_tensor(
            row_mrs[:, :], row_mean[:, :], row_rstd[:, :], ALU.mult),
            name=f"{tag}_mrs", waits=(f"{tag}_rstd",),
            own_waits=(f"{tag}_mean",))
        def bc(tt):
            last = None
            for h in range(2):
                tt.matmul(bc_ps[:, h * 512:(h + 1) * 512], ones1_sb[:, :],
                          row_rstd[0:1, h * 512:(h + 1) * 512],
                          start=True, stop=True)
                last = tt.matmul(bc_ps[:, 1024 + h * 512:1024 + (h + 1) * 512],
                                 ones1_sb[:, :],
                                 row_mrs[0:1, h * 512:(h + 1) * 512],
                                 start=True, stop=True)
            return last
        em.step("PE", bc, name=f"{tag}_bc", waits=(f"{tag}_mrs",))
        em.step("DVE", lambda v: v.tensor_copy(rstdb_sb[:, :], bc_ps[:, 0:1024]),
                name=f"{tag}_rb", waits=(f"{tag}_bc",))
        em.step("DVE", lambda v: v.tensor_copy(mrsb_sb[:, :], bc_ps[:, 1024:2048]),
                name=f"{tag}_mb")

    def ln_apply(em, tag, c, rstdb_sb, mrsb_sb, tscratch, dst, dsl,
                 extra_waits=()):
        em.step("DVE", lambda v: v.tensor_tensor(
            tscratch[:, :], Rh[0][:, c * TOK:(c + 1) * TOK], rstdb_sb[:, :],
            ALU.mult), name=f"{tag}_xa{c}",
            waits=(f"{tag.split('_')[0]}_rb", f"{tag.split('_')[0]}_mb")
            + tuple(extra_waits))
        em.step("DVE", lambda v: v.tensor_tensor(
            dst[:, dsl], tscratch[:, :], mrsb_sb[:, :], ALU.subtract),
            name=f"{tag}_xln{c}", own_waits=(f"{tag}_xa{c}",))

    NWB = 5

    def ffn_phase(em, tag, t, final):
        (lnt, up_sb, dn_sb, dnf_sb, hid_sb, xln_scr, lnsc, t05, out_stg,
         up_ps, dn_ps) = t
        ln_emit(em, tag, lnt)

        def dn_block(hf):
            for oc2 in range(2):
                def mmdn(tt, hf=hf, oc2=oc2):
                    last = None
                    for ic2 in range(8):
                        for h in range(2):
                            last = tt.matmul(
                                dn_ps[oc2 % 2][:, h * 512:(h + 1) * 512],
                                dn_sb[hf % NWB][:, ic2 * 256 + oc2 * 128:
                                              ic2 * 256 + oc2 * 128 + 128],
                                hid_sb[hf % 2][:, ic2 * TOK + h * 512:
                                       ic2 * TOK + (h + 1) * 512],
                                start=(ic2 == 0), stop=(ic2 == 7))
                    return last
                em.step("PE", mmdn, name=f"{tag}_dn{hf}_{oc2}",
                        waits=(f"{tag}_wdn{hf}", f"{tag}_gelu{hf}_7")
                        + ((f"{tag}_acc{hf}_{oc2-1}",) if oc2 >= 1 else
                           ((f"{tag}_acc{hf-1}_1",) if hf >= 1 else ())))
                em.step("DVE", lambda v, hf=hf, oc2=oc2: v.tensor_scalar_mul(
                    t05[oc2 % 2][:, :], dn_ps[oc2 % 2][:, :], 0.5),
                    name=f"{tag}_t05_{hf}_{oc2}",
                    waits=(f"{tag}_dn{hf}_{oc2}",))
                c = 2 * hf + oc2
                if not final:
                    em.step("DVE", lambda v, c=c, oc2=oc2: v.tensor_tensor(
                        Rh[0][:, c * TOK:(c + 1) * TOK],
                        Rh[0][:, c * TOK:(c + 1) * TOK],
                        t05[oc2 % 2][:, :], ALU.add),
                        name=f"{tag}_acc{hf}_{oc2}",
                        waits=(f"{tag}_t05_{hf}_{oc2}",))
                else:
                    em.step("DVE", lambda v, c=c, oc2=oc2: v.tensor_tensor(
                        out_stg[oc2 % 2][:, :], Rh[0][:, c * TOK:(c + 1) * TOK],
                        t05[oc2 % 2][:, :], ALU.add),
                        name=f"{tag}_acc{hf}_{oc2}",
                        waits=(f"{tag}_t05_{hf}_{oc2}",)
                        + ((f"{tag}_yd{hf}_{oc2-1}",) if oc2 >= 1 else
                           ((f"{tag}_yd{hf-1}_1",) if hf >= 1 else ())))
                    em.step("SY", lambda s, c=c, oc2=oc2: s.dma_start(
                        out=y_d[c * 128:(c + 1) * 128, :],
                        in_=out_stg[oc2 % 2][:, :]),
                        name=f"{tag}_yd{hf}_{oc2}",
                        waits=(f"{tag}_acc{hf}_{oc2}",))
            em.mil[f"{tag}_dn_done{hf}"] = em.mil[f"{tag}_dn{hf}_1"]

        for hf in range(FH):
            if not final:
                em.step("GP", lambda g, hf=hf: g.dma_start(
                    out=up_sb[hf % NWB][:, :],
                    in_=up_d[hf * 128:(hf + 1) * 128, :]),
                    name=f"{tag}_wup{hf}",
                    waits=((f"{tag}_up_done{hf-NWB}", f"{tag}_spu{hf-NWB}")
                           if hf >= NWB else ()))
                em.step("SY", lambda s, hf=hf: s.dma_start(
                    out=upbf_d[hf * 128:(hf + 1) * 128, :],
                    in_=up_sb[hf % NWB][:, :]), name=f"{tag}_spu{hf}",
                    waits=(f"{tag}_wup{hf}",))
                em.step("SY", lambda s, hf=hf: s.dma_start(
                    out=dnf_sb[hf % 2][:, :],
                    in_=dn_d[hf * 128:(hf + 1) * 128, :]),
                    name=f"{tag}_wdf{hf}",
                    waits=(f"{tag}_wdn{hf-2}",) if hf >= 2 else ())
                em.step("DVE", lambda v, hf=hf: v.tensor_copy(
                    dn_sb[hf % NWB][:, :], dnf_sb[hf % 2][:, :]),
                    name=f"{tag}_wdn{hf}",
                    waits=(f"{tag}_wdf{hf}",)
                    + ((f"{tag}_dn_done{hf-NWB}", f"{tag}_spd{hf-NWB}")
                       if hf >= NWB else ()))
                em.step("SY", lambda s, hf=hf: s.dma_start(
                    out=dnbf_d[hf * 128:(hf + 1) * 128, :],
                    in_=dn_sb[hf % NWB][:, :]), name=f"{tag}_spd{hf}",
                    waits=(f"{tag}_wdn{hf}",))
            else:
                em.step("GP", lambda g, hf=hf: g.dma_start(
                    out=up_sb[hf % NWB][:, :],
                    in_=upbf_d[hf * 128:(hf + 1) * 128, :]),
                    name=f"{tag}_wup{hf}",
                    waits=(f"{tag}_up_done{hf-NWB}",) if hf >= NWB else ())
                em.step("SY", lambda s, hf=hf: s.dma_start(
                    out=dn_sb[hf % NWB][:, :],
                    in_=dnbf_d[hf * 128:(hf + 1) * 128, :]),
                    name=f"{tag}_wdn{hf}",
                    waits=(f"{tag}_dn_done{hf-NWB}",) if hf >= NWB else ())
            hp = (hf % 2) * 2 * TOK
            if hf == 0:
                for ci in range(2):
                    ln_apply(em, f"{tag}_h0", ci, lnt[-2], lnt[-1],
                             lnsc, xln_scr, slice(ci * TOK, (ci + 1) * TOK))
            for oc in range(8):
                def mmup(tt, hf=hf, oc=oc, hp=hp):
                    last = None
                    for ci in range(2):
                        for h in range(2):
                            last = tt.matmul(
                                up_ps[oc % 2][:, h * 512:(h + 1) * 512],
                                up_sb[hf % NWB][:, ci * 1024 + oc * 128:
                                              ci * 1024 + oc * 128 + 128],
                                xln_scr[:, hp + ci * TOK + h * 512:
                                        hp + ci * TOK + (h + 1) * 512],
                                start=(ci == 0), stop=(ci == 1))
                    return last
                em.step("PE", mmup, name=f"{tag}_up{hf}_{oc}",
                        waits=(f"{tag}_wup{hf}", f"{tag}_h{hf}_xln{2*hf+1}")
                        + ((f"{tag}_gelu{hf}_{oc-2}",) if oc >= 2 else
                           ((f"{tag}_gelu{hf-1}_{oc+6}",) if hf >= 1 else ())))
                em.step("ACT", lambda a, hf=hf, oc=oc: a.activation(
                    hid_sb[hf % 2][:, oc * TOK:(oc + 1) * TOK],
                    up_ps[oc % 2][:, :], AF.Gelu),
                    name=f"{tag}_gelu{hf}_{oc}", waits=(f"{tag}_up{hf}_{oc}",)
                    + ((f"{tag}_dn{hf-2}_1",) if hf >= 2 else ()))
            em.mil[f"{tag}_up_done{hf}"] = em.mil[f"{tag}_up{hf}_7"]
            if hf + 1 < FH:
                hp1 = ((hf + 1) % 2) * 2 * TOK
                for ci in range(2):
                    c = 2 * (hf + 1) + ci
                    ln_apply(em, f"{tag}_h{hf+1}", c, lnt[-2], lnt[-1],
                             lnsc, xln_scr, slice(hp1 + ci * TOK,
                                                  hp1 + (ci + 1) * TOK),
                             extra_waits=((f"{tag}_up{hf-1}_7",)
                                          if hf >= 1 and ci == 0 else ()))
            if hf >= 1:
                dn_block(hf - 1)
        dn_block(FH - 1)
        em.step("SY", lambda s: s.wait_ge(em.sems["DS"], em.c["DS"]) and
                None, inc=0)
        em.step("GP", lambda g: g.wait_ge(em.sems["DW"], em.c["DW"]) and None,
                inc=0)
        em.step("ACT", lambda a: a.wait_ge(em.sems["DA"], em.c["DA"]) and None,
                inc=0)

    def make_ln_tensors(st, pfx, big, big2):
        sq_sb = st.enter_context(nc.sbuf_tensor(pfx + "sq", [128, 2 * TOK], BF16))
        rows = [st.enter_context(nc.sbuf_tensor(pfx + n, [1, TOK], F32))
                for n in ("mean", "ms", "rstd", "mrs")]
        sum_ps = big2[0:1, 0:TOK]
        sq_ps = big2[0:1, TOK:2 * TOK]
        bc_ps = big[:, 0:2 * TOK]
        rstdb = st.enter_context(nc.sbuf_tensor(pfx + "rstdb", [128, TOK], F32))
        mrsb = st.enter_context(nc.sbuf_tensor(pfx + "mrsb", [128, TOK], F32))
        return (sq_sb, *rows, sum_ps, sq_ps, bc_ps, rstdb, mrsb)

    def make_ffn_tensors(st, pfx, stage_dn):
        big = st.enter_context(nc.psum_tensor(pfx + "bigA", [128, 2 * TOK], F32))
        big2 = st.enter_context(nc.psum_tensor(pfx + "bigB", [128, 2 * TOK], F32))
        lnt = make_ln_tensors(st, pfx + "ln", big, big2)
        up_sb = [st.enter_context(nc.sbuf_tensor(f"{pfx}up{i}", [128, 2048], BF16))
                 for i in range(NWB)]
        dn_sb = [st.enter_context(nc.sbuf_tensor(f"{pfx}dn{i}", [128, 2048], BF16))
                 for i in range(NWB)]
        dnf_sb = ([st.enter_context(
            nc.sbuf_tensor(f"{pfx}dnf{i}", [128, 2048], F32))
            for i in range(2)] if stage_dn else None)
        hid_sb = [st.enter_context(
            nc.sbuf_tensor(f"{pfx}hid{i}", [128, 8 * TOK], BF16))
            for i in range(2)]
        xln_scr = st.enter_context(nc.sbuf_tensor(pfx + "xln", [128, 4 * TOK], BF16))
        lnsc = st.enter_context(nc.sbuf_tensor(pfx + "lnsc", [128, TOK], F32))
        t05 = [st.enter_context(nc.sbuf_tensor(f"{pfx}t05{i}", [128, TOK], F32))
               for i in range(2)]
        out_stg = ([st.enter_context(
            nc.sbuf_tensor(f"{pfx}os{i}", [128, TOK], F32))
            for i in range(2)] if not stage_dn else None)
        up_ps = [big[:, i * TOK:(i + 1) * TOK] for i in range(2)]
        dn_ps = [big2[:, i * TOK:(i + 1) * TOK] for i in range(2)]
        return (lnt, up_sb, dn_sb, dnf_sb, hid_sb, xln_scr, lnsc, t05, out_stg,
                up_ps, dn_ps)

    def p2a_make(st):
        big = st.enter_context(nc.psum_tensor("p2bigA", [128, 2 * TOK], F32))
        big2 = st.enter_context(nc.psum_tensor("p2bigB", [128, 2 * TOK], F32))
        lnt = make_ln_tensors(st, "l2", big, big2)
        xchunk = [st.enter_context(nc.sbuf_tensor(f"xc{i}", [128, TOK], BF16))
                  for i in range(2)]
        tsc = st.enter_context(nc.sbuf_tensor("l2t", [128, TOK], F32))
        wstg = [st.enter_context(nc.sbuf_tensor(f"wst{i}", [128, 8192], BF16))
                for i in range(2)]
        return (lnt, xchunk, tsc, wstg)

    def p2b_make(st):
        big = st.enter_context(nc.psum_tensor("p2bigC", [128, 2 * TOK], F32))
        big2 = st.enter_context(nc.psum_tensor("p2bigD", [128, 2 * TOK], F32))
        xg = [st.enter_context(nc.sbuf_tensor(f"xg{i}", [128, NCH * TOK], BF16))
              for i in range(2)]
        wq_sb = [st.enter_context(nc.sbuf_tensor(f"wq{i}", [128, 4096], BF16))
                 for i in range(3)]
        wv_sb = [st.enter_context(nc.sbuf_tensor(f"wv{i}", [128, 8192], BF16))
                 for i in range(2)]
        bv_sb = st.enter_context(nc.sbuf_tensor("bv", [128, 1024], F32))
        qk_stg = [st.enter_context(nc.sbuf_tensor(f"qs{i}", [128, TOK], BF16))
                  for i in range(2)]
        v_stg = [st.enter_context(nc.sbuf_tensor(f"vs{i}", [128, 256], BF16))
                 for i in range(2)]
        qk_ps = [big[:, i * TOK:(i + 1) * TOK] for i in range(2)]
        v_ps = [big2[:, i * 512:i * 512 + 256] for i in range(2)]
        return (xg, wq_sb, wv_sb, bv_sb, qk_stg, v_stg, qk_ps, v_ps)

    def p2a(em, t):
        (lnt, xchunk, tsc, wstg) = t
        for oc in range(16):
            em.step("GP", lambda g, oc=oc: g.dma_start(
                out=wstg[oc % 2][:, 0:4096],
                in_=w1qk_d[oc * 128:(oc + 1) * 128, :]),
                name=f"pcq{oc}",
                waits=(f"wqsp{oc-2}",) if oc >= 2 else ())
            em.step("SY", lambda s, oc=oc: s.dma_start(
                out=wqbf_d[oc * 128:(oc + 1) * 128, :],
                in_=wstg[oc % 2][:, 0:4096]), name=f"wqsp{oc}",
                waits=(f"pcq{oc}",))
        for vg in range(4):
            em.step("GP", lambda g, vg=vg: g.dma_start(
                out=wstg[vg % 2][:, :],
                in_=w1v_d[vg * 128:(vg + 1) * 128, :]),
                name=f"pcv{vg}",
                waits=(f"wvsp{vg-2}",) if vg >= 2 else (f"wqsp{14+vg}",)),
            em.step("SY", lambda s, vg=vg: s.dma_start(
                out=wvbf_d[vg * 128:(vg + 1) * 128, :],
                in_=wstg[vg % 2][:, :]), name=f"wvsp{vg}",
                waits=(f"pcv{vg}",))
        ln_emit(em, "l2", lnt)
        for c in range(NCH):
            ln_apply(em, f"l2_c", c, lnt[-2], lnt[-1], tsc,
                     xchunk[c % 2], slice(0, TOK),
                     extra_waits=(f"l2_sp{c-2}",) if c >= 2 else ())
            em.step("ACT", lambda a, c=c: a.dma_start(
                out=xln_bounce[c * 128:(c + 1) * 128, :],
                in_=xchunk[c % 2][:, :]), name=f"l2_sp{c}", sem="DA",
                waits=(f"l2_c_xln{c}",))
        em.step("GP", lambda g: g.dma_start(
            out=r_spill[:, :].rearrange("(c p) t -> p c t", p=128),
            in_=Rh[0][:, :].rearrange("p (c t) -> p c t", c=NCH)),
            name="r_spill")
        def cc(g, sem):
            g.wait_ge(em.sems[em.mil[f"l2_sp{NCH-1}"][0]],
                      em.mil[f"l2_sp{NCH-1}"][1])
            g.collective_compute(
                "AllGather", ALU.bypass,
                replica_groups=[[0, 1, 2, 3], [4, 5, 6, 7]],
                ins=[xln_bounce[:, :].opt()],
                outs=[xln_gather[:, :].opt()]).then_inc(sem, 1)
        em.raw("GP", cc, name="cc1", inc=1, sem="CC1")

    def p2b(em, t):
        (xg, wq_sb, wv_sb, bv_sb, qk_stg, v_stg, qk_ps, v_ps) = t
        em.step("SY", lambda s: s.dma_start(out=bv_sb[:, :], in_=bv_d[:, :]),
                name="ld_bv")
        for oc in range(3):
            em.step("ACT", lambda a, oc=oc: a.dma_start(
                out=wq_sb[oc % 3][:, :],
                in_=wqbf_d[oc * 128:(oc + 1) * 128, :]),
                name=f"wq{oc}", sem="DA", waits=(f"wqsp{oc}",))
        for vg in range(2):
            em.step("ACT", lambda a, vg=vg: a.dma_start(
                out=wv_sb[vg % 2][:, :],
                in_=wvbf_d[vg * 128:(vg + 1) * 128, :]),
                name=f"wv{vg}", sem="DA", waits=(f"wvsp{vg}",))
        gap = xln_gather[:, :].rearrange("(tt c p) t -> tt p c t",
                                         tt=GROUP, p=128)
        xgv = [x[:, :].rearrange("p (c t) -> p c t", c=NCH) for x in xg]
        for tt in range(GROUP):
            em.step("SY", lambda s, tt=tt: s.dma_start(
                out=xgv[tt % 2], in_=gap[tt, :, :, :]), name=f"xg{tt}",
                waits=("cc1",)
                + ((f"p2_vdone{tt-2}",) if tt >= 2 and _P2V else
                   ((f"qkmm{tt*16-17}",) if tt >= 2 and
                    f"qkmm{tt*16-17}" in em.mil else ())))
            for oc in range(16):
                i = tt * 16 + oc
                if i >= _P2QK:
                    continue
                def mmqk(tt_, i=i, oc=oc):
                    last = None
                    for ic in range(NCH):
                        for h in range(2):
                            last = tt_.matmul(
                                qk_ps[i % 2][:, h * 512:(h + 1) * 512],
                                wq_sb[i % 3][:, ic * 128:(ic + 1) * 128],
                                xg[tt % 2][:, ic * TOK + h * 512:
                                           ic * TOK + (h + 1) * 512],
                                start=(ic == 0), stop=(ic == NCH - 1))
                    return last
                em.step("PE", mmqk, name=f"qkmm{i}",
                        waits=(f"xg{tt}", f"wq{i}")
                        + ((f"qkev{i-2}",) if i >= 2 else ()))
                em.step("DVE", lambda v, i=i, oc=oc: v.tensor_scalar_add(
                    qk_stg[i % 2][:, :], qk_ps[i % 2][:, :],
                    bqk_sb[:, oc:oc + 1]), name=f"qkev{i}",
                    waits=(f"qkmm{i}",)
                    + ((f"qksp{i-2}",) if i >= 2 else ()))
                em.step("ACT", lambda a, i=i, oc=oc, tt=tt: a.dma_start(
                    out=qk_spill[oc * 128:(oc + 1) * 128,
                                 tt * TOK:(tt + 1) * TOK],
                    in_=qk_stg[i % 2][:, :]), name=f"qksp{i}", sem="DA",
                    waits=(f"qkev{i}",))
                if i + 3 < 64:
                    em.step("ACT", lambda a, i=i: a.dma_start(
                        out=wq_sb[(i + 3) % 3][:, :],
                        in_=wqbf_d[((i + 3) % 16) * 128:
                                   ((i + 3) % 16 + 1) * 128, :]),
                        name=f"wq{i+3}", sem="DA",
                        waits=(f"qkmm{i}", f"wqsp{(i+3) % 16}"))
            for vg in range(4 if _P2V else 0):
                gi = tt * 4 + vg
                for ttile in range(8):
                    vi = gi * 8 + ttile
                    def mmv(tt_, tt=tt, ttile=ttile, vi=vi, gi=gi):
                        last = None
                        for ic in range(NCH):
                            last = tt_.matmul(
                                v_ps[ttile % 2][:, :],
                                xg[tt % 2][:, ic * TOK + ttile * 128:
                                           ic * TOK + ttile * 128 + 128],
                                wv_sb[gi % 2][:, ic * 256:(ic + 1) * 256],
                                start=(ic == 0), stop=(ic == NCH - 1))
                        return last
                    em.step("PE", mmv, name=f"vmm{gi}_{ttile}",
                            waits=(f"wv{gi}", f"xg{tt}")
                            + ((f"vev{vi-2}",) if vi >= 2 else ()))
                    em.step("DVE", lambda v, vi=vi, vg=vg, ttile=ttile:
                            v.tensor_tensor(
                                v_stg[ttile % 2][:, :], v_ps[ttile % 2][:, :],
                                bv_sb[:, vg * 256:(vg + 1) * 256], ALU.add),
                            name=f"vev{vi}",
                            waits=(f"vmm{gi}_{ttile}", "ld_bv")
                            + ((f"vsp{vi-2}",) if vi >= 2 else ()))
                    em.step("ACT", lambda a, tt=tt, ttile=ttile, vg=vg, vi=vi:
                            a.dma_start(
                                out=v_spill[tt * TOK + ttile * 128:
                                            tt * TOK + (ttile + 1) * 128,
                                            vg * 256:(vg + 1) * 256],
                                in_=v_stg[ttile % 2][:, :]),
                            name=f"vsp{vi}", sem="DA", waits=(f"vev{vi}",))
                if ttile == 7 and gi + 2 < 16:
                    em.step("ACT", lambda a, gi=gi: a.dma_start(
                        out=wv_sb[gi % 2][:, :],
                        in_=wvbf_d[((gi + 2) % 4) * 128:
                                   ((gi + 2) % 4 + 1) * 128, :]),
                        name=f"wv{gi+2}", sem="DA",
                        waits=(f"vmm{gi}_7", f"wvsp{(gi+2) % 4}"))
            if _P2V:
                em.mil[f"p2_vdone{tt}"] = em.mil[f"vmm{tt*4+3}_7"]
        em.step("GP", lambda g: g.wait_ge(em.sems["DW"], em.c["DW"]) and None,
                inc=0)
        em.step("SY", lambda s: s.wait_ge(em.sems["DS"], em.c["DS"]) and None,
                inc=0)
        em.step("ACT", lambda a: a.wait_ge(em.sems["DA"], em.c["DA"]) and None,
                inc=0)

    NB = 4
    DEPTH = 3

    def p3_make(st):
        q_sb = [st.enter_context(nc.sbuf_tensor(f"q{i}", [128, S], BF16))
                for i in range(2)]
        k_sb = [st.enter_context(nc.sbuf_tensor(f"k{i}", [128, S], BF16))
                for i in range(2)]
        v_sb = st.enter_context(nc.sbuf_tensor("vall", [128, 32 * 1024], BF16))
        e_sb = st.enter_context(nc.sbuf_tensor("e_sb", [128, 8 * 512], F32))
        pbar = [st.enter_context(nc.sbuf_tensor(f"pb{i}", [128, 512], BF16))
                for i in range(NB)]
        nsc = [st.enter_context(nc.sbuf_tensor(f"nsc{i}", [128, 512], F32))
               for i in range(4)]
        o_stg = [st.enter_context(nc.sbuf_tensor(f"og{i}", [128, 512], BF16))
                 for i in range(2)]
        s_ps = [st.enter_context(nc.psum_tensor(f"sps{i}", [128, 512], F32))
                for i in range(NB)]
        o_ps = [st.enter_context(nc.psum_tensor(f"ops{i}", [128, 512], F32))
                for i in range(2)]
        l_ps = [st.enter_context(nc.psum_tensor(f"lps{i}", [128, 512], F32))
                for i in range(2)]
        return (q_sb, k_sb, v_sb, e_sb, pbar, nsc, o_stg, s_ps, o_ps, l_ps)

    def p3(em, t):
        (q_sb, k_sb, v_sb, e_sb, pbar, nsc, o_stg, s_ps, o_ps, l_ps) = t
        em.step("SY", lambda s: s.dma_start(out=e_sb[:, :], in_=etab_d[:, :]),
                name="ld_e")
        vap = v_spill[:, :].rearrange("(kt p) o -> p kt o", p=128)
        vsv = v_sb[:, :].rearrange("p (kt o) -> p kt o", o=1024)
        for vc in range(4):
            em.step("GP" if vc % 2 == 0 else "SY",
                    lambda g, vc=vc: g.dma_start(
                        out=vsv[:, vc * 8:(vc + 1) * 8, :],
                        in_=vap[:, vc * 8:(vc + 1) * 8, :]),
                    name=f"a_v{vc}")
        for jj in range(2):
            em.step("SY", lambda s, jj=jj: s.dma_start(
                out=q_sb[jj][:, :],
                in_=qk_spill[jj * 128:(jj + 1) * 128, :]), name=f"a_q{jj}")
            em.step("SY", lambda s, jj=jj: s.dma_start(
                out=k_sb[jj][:, :],
                in_=qk_spill[(8 + jj) * 128:(9 + jj) * 128, :]),
                name=f"a_k{jj}")
        flat = []
        for j in range(NSLOT):
            for qc in range(8):
                items = sched[j][qc]
                for it, (kt, qlo, sl) in enumerate(items):
                    flat.append((j, qc, it, len(items), kt, qlo, sl))
        nflat = len(flat)

        def emit_st(g):
            j, qc, it, nk, kt, qlo, sl = flat[g]
            qhi = sl[-1][1]
            if qc == 0 and it == 0 and 2 <= j + 1 < NSLOT:
                jn = j + 1
                em.step("SY", lambda s, jn=jn: s.dma_start(
                    out=q_sb[jn % 2][:, :],
                    in_=qk_spill[jn * 128:(jn + 1) * 128, :]),
                    name=f"a_q{jn}", waits=(f"a_st_last{jn-2}",))
                em.step("SY", lambda s, jn=jn: s.dma_start(
                    out=k_sb[jn % 2][:, :],
                    in_=qk_spill[(8 + jn) * 128:(9 + jn) * 128, :]),
                    name=f"a_k{jn}", waits=(f"a_st_last{jn-2}",))
            def mmst(tt, j=j, qc=qc, kt=kt, qlo=qlo, qhi=qhi, g=g):
                return tt.matmul(
                    s_ps[g % NB][:, qlo:qhi],
                    k_sb[j % 2][:, kt * 128:(kt + 1) * 128],
                    q_sb[j % 2][:, qc * 512 + qlo:qc * 512 + qhi],
                    start=True, stop=True)
            em.step("PE", mmst, name=f"a_stG{g}",
                    waits=(f"a_q{j}", f"a_k{j}")
                    + ((f"a_expG{g-NB}",) if g >= NB else ()))
            if g == flat_last_of_slot[j]:
                em.mil[f"a_st_last{j}"] = em.mil[f"a_stG{g}"]
            diag = kt >= 4 * qc
            if diag:
                dq = 128 * (kt - 4 * qc)
                em.step("DVE", lambda v, g=g, dq=dq: v.tensor_tensor(
                    s_ps[g % NB][:, dq:dq + 128],
                    s_ps[g % NB][:, dq:dq + 128],
                    mask_sb[:, :], ALU.add),
                    name=f"a_maskG{g}", waits=(f"a_stG{g}",))
            def expf(a, j=j, sl=sl, g=g):
                last = None
                for (aa, bb, dk) in sl:
                    bi = bkeys[(j, dk)]
                    last = a.activation(
                        pbar[g % NB][:, aa:bb], s_ps[g % NB][:, aa:bb],
                        AF.Exp, bias=btab_sb[:, bi:bi + 1],
                        scale=SCALE)
                return last
            em.step("ACT", expf, name=f"a_expG{g}",
                    waits=((f"a_maskG{g}",) if diag else (f"a_stG{g}",))
                    + ("ld_consts",)
                    + ((f"a_avG{g-NB}",) if g >= NB else ()))

        def emit_av(g):
            j, qc, it, nk, kt, qlo, sl = flat[g]
            qhi = sl[-1][1]
            first = (it == 0)
            last_it = (it == nk - 1)
            def mmav(tt, j=j, qc=qc, kt=kt, qlo=qlo, qhi=qhi, g=g,
                     first=first, last_it=last_it):
                tt.matmul(o_ps[qc % 2][:, qlo:qhi],
                          v_sb[:, kt * 1024 + j * 128:
                               kt * 1024 + (j + 1) * 128],
                          pbar[g % NB][:, qlo:qhi],
                          start=first, stop=last_it)
                return tt.matmul(l_ps[qc % 2][:, qlo:qhi],
                                 ones_bf[:, :],
                                 pbar[g % NB][:, qlo:qhi],
                                 start=first, stop=last_it)
            em.step("PE", mmav, name=f"a_avG{g}",
                    waits=(f"a_expG{g}", f"a_v{kt//8}")
                    + ((f"a_norm_done{j}_{qc-2}",) if first and qc >= 2
                       else ())
                    + ((f"a_normprev{j}_{qc}",) if first and
                       f"a_normprev{j}_{qc}" in em.mil else ()))
            if not last_it:
                return
            war_le = ((f"a_rp{j}_{qc-2}",) if qc >= 2 else
                      ((f"a_rp{j-1}_{qc+6}",) if j >= 1 else ()))
            war_rp = ((f"a_on{j}_{qc-2}",) if qc >= 2 else
                      ((f"a_on{j-1}_{qc+6}",) if j >= 1 else ()))
            em.step("DVE", lambda v, j=j, qc=qc: v.tensor_tensor(
                nsc[(qc % 2) * 2][:, :], l_ps[qc % 2][:, :],
                e_sb[:, j * 512:(j + 1) * 512], ALU.mult),
                name=f"a_le{j}_{qc}",
                waits=(f"a_avG{g}", "ld_e") + war_le)
            em.step("ACT", lambda a, j=j, qc=qc: a.activation(
                nsc[(qc % 2) * 2][:, :], nsc[(qc % 2) * 2][:, :], AF.Ln),
                name=f"a_ln{j}_{qc}", waits=(f"a_le{j}_{qc}",))
            em.step("ACT", lambda a, j=j, qc=qc: a.activation(
                nsc[(qc % 2) * 2 + 1][:, :], nsc[(qc % 2) * 2][:, :],
                AF.Exp, scale=-1.0), name=f"a_rp{j}_{qc}",
                own_waits=(f"a_ln{j}_{qc}",), waits=war_rp)
            em.step("DVE", lambda v, j=j, qc=qc: v.tensor_tensor(
                nsc[(qc % 2) * 2 + 1][:, :], nsc[(qc % 2) * 2 + 1][:, :],
                e_sb[:, j * 512:(j + 1) * 512], ALU.mult),
                name=f"a_r{j}_{qc}", waits=(f"a_rp{j}_{qc}",))
            em.step("DVE", lambda v, j=j, qc=qc: v.tensor_tensor(
                o_stg[qc % 2][:, :], o_ps[qc % 2][:, :],
                nsc[(qc % 2) * 2 + 1][:, :], ALU.mult),
                name=f"a_on{j}_{qc}", own_waits=(f"a_r{j}_{qc}",),
                waits=((f"a_od{j}_{qc-2}",) if qc >= 2 else
                       ((f"a_od{j-1}_{qc+6}",) if j >= 1 else ())))
            em.mil[f"a_norm_done{j}_{qc}"] = em.mil[f"a_on{j}_{qc}"]
            if qc == 6:
                em.mil[f"a_normprev{j+1}_0"] = em.mil[f"a_norm_done{j}_6"]
            if qc == 7:
                em.mil[f"a_normprev{j+1}_1"] = em.mil[f"a_norm_done{j}_7"]
            em.step("SY", lambda s, j=j, qc=qc: s.dma_start(
                out=o_bounce[j * 128:(j + 1) * 128,
                             qc * 512:(qc + 1) * 512],
                in_=o_stg[qc % 2][:, :]), name=f"a_od{j}_{qc}",
                waits=(f"a_on{j}_{qc}",))
            if qc == 7:
                def ccj(g, sem, j=j):
                    g.wait_ge(em.sems["DS"], em.mil[f"a_od{j}_7"][1])
                    g.collective_compute(
                        "AllGather", ALU.bypass,
                        replica_groups=[[0, 1, 2, 3], [4, 5, 6, 7]],
                        ins=[o_bounce[j * 128:(j + 1) * 128, :].opt()],
                        outs=[o_gather[j * 512:(j + 1) * 512, :].opt()]
                    ).then_inc(sem, 1)
                em.raw("GP", ccj, name=f"cc2_{j}", inc=1, sem="CC2")

        flat_last_of_slot = {}
        for g, tile in enumerate(flat):
            flat_last_of_slot[tile[0]] = g
        for pos in range(nflat + DEPTH):
            if pos < nflat:
                emit_st(pos)
            if pos >= DEPTH:
                emit_av(pos - DEPTH)
        em.step("GP", lambda g: g.wait_ge(em.sems["DW"], em.c["DW"]) and None,
                inc=0)
        em.step("SY", lambda s: s.wait_ge(em.sems["DS"], em.c["DS"]) and None,
                inc=0)

    def p4_make(st):
        o_own = st.enter_context(nc.sbuf_tensor("oown", [128, NCH * TOK], BF16))
        w2_sb = [st.enter_context(nc.sbuf_tensor(f"w2_{i}", [128, 4096], BF16))
                 for i in range(3)]
        w2f = [st.enter_context(nc.sbuf_tensor(f"w2f{i}", [128, 3072], F32))
               for i in range(2)]
        t05 = [st.enter_context(nc.sbuf_tensor(f"p4t{i}", [128, TOK], F32))
               for i in range(2)]
        op_ps = [st.enter_context(nc.psum_tensor(f"opp{i}", [128, TOK], F32))
                 for i in range(2)]
        return (o_own, w2_sb, w2f, t05, op_ps)

    def p4(em, t):
        (o_own, w2_sb, w2f, t05, op_ps) = t
        em.step("SY", lambda s: s.dma_start(
            out=Rh[0][:, :].rearrange("p (c t) -> p c t", c=NCH),
            in_=r_spill[:, :].rearrange("(c p) t -> p c t", p=128)),
            name="p4_rld", waits=("r_spill",))
        def ldo(g, sem):
            g.wait_ge(em.sems["CC2"], NSLOT)
            pid = nc.partition_id([ET.Pool])
            reg = g.compute_val(pid)
            for i in g.Switch(reg, 8):
                rr = i % 4
                for ic in range(NCH):
                    g.dma_start(
                        out=o_own[:, ic * TOK:(ic + 1) * TOK],
                        in_=o_gather[ic * 128:(ic + 1) * 128,
                                     rr * TOK:(rr + 1) * TOK]
                    ).then_inc(sem, 16)
        em.raw("GP", ldo, name="p4_oo", inc=16 * NCH, sem="DW")
        def emit_wa(oc):
            em.step("GP", lambda g, oc=oc: g.dma_start(
                out=w2_sb[oc % 3][:, 0:1024],
                in_=w2_d[oc * 128:(oc + 1) * 128, 0:1024]), name=f"p4_wa{oc}",
                waits=(f"p4_mm{oc-3}",) if oc >= 3 else ())

        def emit_wf(oc):
            em.step("SY", lambda s, oc=oc: s.dma_start(
                out=w2f[oc % 2][:, :],
                in_=w2_d[oc * 128:(oc + 1) * 128, 1024:4096]),
                name=f"p4_wf{oc}",
                waits=(f"p4_cv{oc-2}",) if oc >= 2 else ())

        def emit_cv(oc):
            em.step("DVE", lambda v, oc=oc: v.tensor_copy(
                w2_sb[oc % 3][:, 1024:4096], w2f[oc % 2][:, :]),
                name=f"p4_cv{oc}",
                waits=(f"p4_wf{oc}",)
                + ((f"p4_mm{oc-3}",) if oc >= 3 else ()))

        emit_wa(0)
        emit_wa(1)
        emit_wf(0)
        emit_wf(1)
        emit_cv(0)
        for oc in range(NCH):
            if oc + 2 < NCH:
                emit_wa(oc + 2)
                emit_wf(oc + 2)
            if oc + 1 < NCH:
                emit_cv(oc + 1)
            def mmo(tt, oc=oc):
                last = None
                for ic in range(NCH):
                    for h in range(2):
                        last = tt.matmul(
                            op_ps[oc % 2][:, h * 512:(h + 1) * 512],
                            w2_sb[oc % 3][:, ic * 128:(ic + 1) * 128],
                            o_own[:, ic * TOK + h * 512:ic * TOK + (h + 1) * 512],
                            start=(ic == 0), stop=(ic == NCH - 1))
                return last
            em.step("PE", mmo, name=f"p4_mm{oc}",
                    waits=("p4_oo", f"p4_wa{oc}", f"p4_cv{oc}")
                    + ((f"p4_t{oc-2}",) if oc >= 2 else ()))
            em.step("ACT", lambda a, oc=oc: a.activation(
                t05[oc % 2][:, :], op_ps[oc % 2][:, :], AF.Identity,
                bias=bout_sb[:, oc:oc + 1]), name=f"p4_t{oc}",
                waits=(f"p4_mm{oc}",)
                + ((f"p4_acc{oc-2}",) if oc >= 2 else ()))
            em.step("DVE", lambda v, oc=oc: v.tensor_tensor(
                Rh[0][:, oc * TOK:(oc + 1) * TOK], Rh[0][:, oc * TOK:(oc + 1) * TOK],
                t05[oc % 2][:, :], ALU.add), name=f"p4_acc{oc}",
                waits=(f"p4_t{oc}", "p4_rld"))
        em.step("GP", lambda g: g.wait_ge(em.sems["DW"], em.c["DW"]) and None,
                inc=0)
        em.step("SY", lambda s: s.wait_ge(em.sems["DS"], em.c["DS"]) and None,
                inc=0)

    def pdump_spill(em):
        em.step("GP", lambda g: g.dma_start(out=y_d[:, :], in_=r_spill[:, :]))
        em.step("GP", lambda g: g.wait_ge(em.sems["DW"], em.c["DW"]) and None,
                inc=0)

    def pdump(em):
        for c in range(NCH):
            em.step("GP", lambda g, c=c: g.dma_start(
                out=y_d[c * 128:(c + 1) * 128, :],
                in_=Rh[0][:, c * TOK:(c + 1) * TOK]), name=f"dump{c}")
        if _DUMPQKV:
            em.step("SY", lambda s: s.dma_start(out=dbgq_d[:, :],
                                                in_=qk_spill[:, :]))
            em.step("SY", lambda s: s.dma_start(out=dbgv_d[:, :],
                                                in_=v_spill[:, :]))
        if _DUMPO:
            em.step("SY", lambda s: s.dma_start(out=dbgo_d[:, :],
                                                in_=o_bounce[:, :]))
        if _DUMPQKV or _DUMPO:
            em.step("SY", lambda s: s.wait_ge(em.sems["DS"], em.c["DS"])
                    and None, inc=0)
        em.step("GP", lambda g: g.wait_ge(em.sems["DW"], em.c["DW"]) and None,
                inc=0)

    with stack:
        with ExitStack() as stA:
            Rh[0] = stA.enter_context(
                nc.sbuf_tensor("R", [128, NCH * TOK], BF16))
            run_phase("p0", p0)
            if _PHASES >= 1:
                with ExitStack() as st:
                    t1 = make_ffn_tensors(st, "f1", True)
                    run_phase("p1",
                              lambda em: ffn_phase(em, "f1", t1, final=False))
            if _PHASES >= 2:
                with ExitStack() as st:
                    t2 = p2a_make(st)
                    run_phase("p2a", lambda em: p2a(em, t2))
            if _PHASES <= 1:
                run_phase("pdump", pdump)
        Rh[0] = None
        if _PHASES >= 2:
            with ExitStack() as st:
                t2 = p2b_make(st)
                run_phase("p2b", lambda em: p2b(em, t2))
        if _PHASES >= 3:
            with ExitStack() as st:
                t3 = p3_make(st)
                run_phase("p3", lambda em: p3(em, t3))
        with ExitStack() as stB:
            if _PHASES >= 4:
                Rh[0] = stB.enter_context(
                    nc.sbuf_tensor("R2", [128, NCH * TOK], BF16))
                with ExitStack() as st:
                    t4 = p4_make(st)
                    run_phase("p4", lambda em: p4(em, t4))
            if _PHASES >= 5:
                with ExitStack() as st:
                    t5 = make_ffn_tensors(st, "f5", False)
                    run_phase("p5",
                              lambda em: ffn_phase(em, "f5", t5, final=True))
            elif _PHASES >= 4:
                run_phase("pdump", pdump)
            elif _PHASES >= 2:
                run_phase("pdump_s", pdump_spill)
    return nc



def _stage_inputs(x, wqkv, bqkv, wout, bout, up, down, perm):
    sched = attn_schedule()
    bkeys = bias_keys(sched)
    nbk = len(bkeys)
    slopes = _slopes()
    wqkv_p = wqkv[:, perm]
    w2t = np.ascontiguousarray(wout[perm, :].T)
    bout_p = bout[perm]

    def tileK(a, kchunks, width):
        return np.ascontiguousarray(
            a.reshape(kchunks, 128, width).transpose(1, 0, 2)
            .reshape(128, kchunks * width))

    up_img = np.concatenate([tileK(up[h], 2, 1024)[None] for h in range(FH)],
                            axis=0).reshape(FH * 128, 2048)
    dn_img = np.concatenate([tileK(down[h], 8, 256)[None] for h in range(FH)],
                            axis=0).reshape(FH * 128, 2048)
    mask = np.where(np.arange(128)[:, None] <= np.arange(128)[None, :],
                    0.0, NEG).astype(np.float32)
    onesc = np.ones((128, 128), np.float32)
    ones1 = np.ones((1, 128), np.float32)

    ins = []
    for c in range(8):
        b, r = c // 4, c % 4
        heads = [4 * jj + r for jj in range(NSLOT)]
        x_in = np.ascontiguousarray(x[b, TOK * r:TOK * (r + 1), :][:, perm].T)
        qk_blocks = []
        for jj in range(NSLOT):
            h = heads[jj]
            A = wqkv_p[128 * h:128 * h + 128, :]
            qk_blocks.append(tileK(np.ascontiguousarray(A.T), 32, 128))
        for jj in range(NSLOT):
            h = heads[jj]
            A = wqkv_p[4096 + 128 * h:4096 + 128 * h + 128, :]
            qk_blocks.append(tileK(np.ascontiguousarray(A.T), 32, 128))
        w1qk = np.concatenate([bb[None] for bb in qk_blocks], axis=0)\
            .reshape(16 * 128, 4096)
        W1v = np.ascontiguousarray(np.concatenate(
            [wqkv_p[8192 + 128 * h:8192 + 128 * h + 128, :] for h in heads],
            axis=0).T)
        w1v = np.concatenate(
            [tileK(np.ascontiguousarray(W1v[:, 256 * vg:256 * (vg + 1)]),
                   32, 256)[None] for vg in range(4)],
            axis=0).reshape(4 * 128, 8192)
        w2_ord = np.empty_like(w2t)
        for ic in range(NCH):
            h = 4 * (ic // 4) + ic % 4
            w2_ord[128 * ic:128 * (ic + 1), :] = w2t[128 * h:128 * (h + 1), :]
        w2_img = np.concatenate(
            [tileK(np.ascontiguousarray(w2_ord[:, 128 * oc:128 * (oc + 1)]),
                   32, 128)[None] for oc in range(NCH)],
            axis=0).reshape(32 * 128, 4096)
        bqk = np.empty((128, 16), np.float32)
        for jj in range(NSLOT):
            bqk[:, jj] = bqkv[128 * heads[jj]:128 * heads[jj] + 128]
            bqk[:, 8 + jj] = bqkv[4096 + 128 * heads[jj]:
                                  4096 + 128 * heads[jj] + 128]
        bv_ord = np.concatenate(
            [bqkv[8192 + 128 * h:8192 + 128 * h + 128] for h in heads])
        bvb = np.ascontiguousarray(
            np.broadcast_to(bv_ord[None, :], (128, 1024))).astype(np.float32)
        bout_img = np.ascontiguousarray(
            bout_p.reshape(32, 128).T).astype(np.float32)
        btab = np.zeros((128, nbk), np.float32)
        for (jj, dk), idx in bkeys.items():
            btab[:, idx] = (slopes[heads[jj]]
                            * (np.arange(128) + dk)).astype(np.float32)
        etab = np.zeros((128, 8 * 512), np.float32)
        qq = np.arange(512)
        for jj in range(NSLOT):
            W = W_SLOT[jj]
            qref = (qq // W + 1) * W - 1
            ev = np.exp(slopes[heads[jj]] * (qref - qq)).astype(np.float32)
            etab[:, jj * 512:(jj + 1) * 512] = ev[None, :]
        ins.append({
            "x_in": x_in, "w1qk": w1qk, "w1v": w1v, "w2": w2_img,
            "up_w": up_img, "dn_w": dn_img, "bqk": bqk, "bvb": bvb,
            "bout": bout_img, "btab": btab, "etab": etab, "maskt": mask,
            "onesc": onesc, "onesr": ones1,
            "onecol": np.ones((128, 1), np.float32),
        })
    return ins, nbk


def kernel(L0_attn_x, L0_attn_w_qkv, L0_attn_b_qkv, L0_attn_w_out,
           L0_attn_b_out, L0_ffn_up, L0_ffn_down, L0_ffn_perm):
    x = np.asarray(L0_attn_x, np.float32)
    wqkv = np.asarray(L0_attn_w_qkv, np.float32)
    bqkv = np.asarray(L0_attn_b_qkv, np.float32)
    wout = np.asarray(L0_attn_w_out, np.float32)
    bout = np.asarray(L0_attn_b_out, np.float32)
    up = np.asarray(L0_ffn_up, np.float32)
    down = np.asarray(L0_ffn_down, np.float32)
    perm = np.asarray(L0_ffn_perm).astype(np.int64)
    unperm = np.argsort(perm)

    ins, nbk = _stage_inputs(x, wqkv, bqkv, wout, bout, up, down, perm)
    nc = build_program(nbk)
    res = run_bass_kernel_spmd(nc, ins, core_ids=list(range(8)))

    out = np.empty((B, S, D), np.float32)
    for c in range(8):
        b, r = c // 4, c % 4
        y = np.asarray(res.results[c]["y"], np.float32)
        out[b, TOK * r:TOK * (r + 1), :] = y.T[:, unperm]
    return out


# version 12
# speedup vs baseline: 1.0543x; 1.0543x over previous
import math
import numpy as np

import concourse.bass as bass
import concourse.mybir as mybir
from concourse.bass_utils import run_bass_kernel_spmd

F32 = mybir.dt.float32
BF16 = mybir.dt.bfloat16
AF = mybir.ActivationFunctionType
ALU = mybir.AluOpType
ET = mybir.EngineType

B, S, D = 2, 4096, 4096
NHEAD, HD = 32, 128
GROUP = 4
TOK = S // GROUP
NCH = D // 128
NSLOT = 8
SCALE = 1.0 / math.sqrt(HD)
FH, FDH, FM = 16, 256, 1024
EPS = 1e-5
NEG = -1.0e9

_PHASES = 5
_P2QK = 64
_P2V = True
_DUMPQKV = False
_DUMPO = False

W_SLOT = [64, 128, 256, 512, 512, 512, 512, 512]
SLOPE_MIN = [2.0 ** (-(j + 1)) for j in range(8)]
SKIP_MARGIN = 95.0


def _slopes():
    m = (np.arange(1, 33) * (8.0 / 32)).astype(np.float64)
    return 1.0 / (2.0 ** m)


def attn_schedule():
    sched = []
    for j in range(NSLOT):
        W = W_SLOT[j]
        smin = SLOPE_MIN[j]
        per_qc = []
        for qc in range(8):
            items = []
            for kt in range(4 * qc + 4):
                qlo_c = max(0, 128 * kt - 512 * qc)
                sl = []
                a0 = (qlo_c // W) * W
                for a in range(a0, 512, W):
                    b = a + W
                    aa = max(a, qlo_c)
                    if aa >= b:
                        continue
                    dist = (512 * qc + aa) - (128 * kt + 127)
                    if dist * smin > SKIP_MARGIN:
                        continue
                    qref = 512 * qc + b - 1
                    sl.append((aa, b, 128 * kt - qref))
                if sl:
                    items.append((kt, sl[0][0], sl))
            per_qc.append(items)
        sched.append(per_qc)
    return sched


def bias_keys(sched):
    keys = []
    seen = set()
    for j in range(NSLOT):
        for qc in range(8):
            for _, _, sl in sched[j][qc]:
                for _, _, d in sl:
                    if (j, d) not in seen:
                        seen.add((j, d))
                        keys.append((j, d))
    return {k: i for i, k in enumerate(keys)}



ENGS = ("PE", "ACT", "DVE", "GP", "SY")
DEF_SEM = {"PE": "PE", "ACT": "ACT", "DVE": "DVE", "GP": "DW", "SY": "DS"}


class Em:
    def __init__(self, sems):
        self.sems = sems
        self.c = {k: 0 for k in sems}
        self.mil = {}
        self.engine = "SIM"
        self.eng = None
        self.last_wait = {}

    def snap(self):
        return dict(self.c)

    def restore(self, snap):
        self.c = dict(snap)
        self.last_wait = {}

    def wait(self, *names):
        for n in names:
            semname, val = self.mil[n]
            if self.eng is None:
                continue
            if DEF_SEM.get(self.engine) == semname:
                continue
            if self.last_wait.get(semname, -1) >= val:
                continue
            self.last_wait[semname] = val
            self.eng.wait_ge(self.sems[semname], val)

    def wait_own(self, name):
        semname, val = self.mil[name]
        if self.eng is not None:
            if self.last_wait.get(semname, -1) < val:
                self.last_wait[semname] = val
                self.eng.wait_ge(self.sems[semname], val)

    def step(self, engine, fn, name=None, waits=(), inc=None, sem=None,
             own_waits=()):
        sem = sem or DEF_SEM[engine]
        inc = inc if inc is not None else (16 if sem in ("DW", "DS", "DA")
                                           else 1)
        self.c[sem] += inc
        if name is not None:
            prev = self.mil.get(name)
            assert prev is None or prev == (sem, self.c[sem]), (name, prev)
            self.mil[name] = (sem, self.c[sem])
        if self.engine != engine:
            return
        self.wait(*waits)
        for w in own_waits:
            self.wait_own(w)
        last = fn(self.eng)
        if last is not None:
            last.then_inc(self.sems[sem], inc)

    def raw(self, engine, fn, name=None, waits=(), inc=0, sem=None):
        sem = sem or DEF_SEM[engine]
        self.c[sem] += inc
        if name is not None:
            self.mil[name] = (sem, self.c[sem])
        if self.engine != engine:
            return
        self.wait(*waits)
        fn(self.eng, self.sems[sem])



def build_program(nbk):
    sched = attn_schedule()
    bkeys = bias_keys(sched)
    assert nbk == len(bkeys)

    nc = bass.Bass("TRN2", target_bir_lowering=False)

    x_d = nc.dram_tensor("x_in", [D, TOK], F32, kind="ExternalInput")
    w1qk_d = nc.dram_tensor("w1qk", [16 * 128, 4096], F32, kind="ExternalInput")
    w1v_d = nc.dram_tensor("w1v", [4 * 128, 8192], F32, kind="ExternalInput")
    w2_d = nc.dram_tensor("w2", [32 * 128, 4096], F32, kind="ExternalInput")
    up_d = nc.dram_tensor("up_w", [16 * 128, 2048], F32, kind="ExternalInput")
    dn_d = nc.dram_tensor("dn_w", [16 * 128, 2048], F32, kind="ExternalInput")
    bqk_d = nc.dram_tensor("bqk", [128, 16], F32, kind="ExternalInput")
    bv_d = nc.dram_tensor("bvb", [128, 1024], F32, kind="ExternalInput")
    bout_d = nc.dram_tensor("bout", [128, 32], F32, kind="ExternalInput")
    btab_d = nc.dram_tensor("btab", [128, nbk], F32, kind="ExternalInput")
    etab_d = nc.dram_tensor("etab", [128, 8 * 512], F32, kind="ExternalInput")
    mask_d = nc.dram_tensor("maskt", [128, 128], F32, kind="ExternalInput")
    onesc_d = nc.dram_tensor("onesc", [128, 128], F32, kind="ExternalInput")
    onecol_d = nc.dram_tensor("onecol", [128, 1], F32, kind="ExternalInput")
    ones1_d = nc.dram_tensor("onesr", [1, 128], F32, kind="ExternalInput")
    y_d = nc.dram_tensor("y", [D, TOK], F32, kind="ExternalOutput")
    if _DUMPQKV:
        dbgq_d = nc.dram_tensor("dbgq", [16 * 128, S], BF16,
                                kind="ExternalOutput")
        dbgv_d = nc.dram_tensor("dbgv", [S, 1024], BF16,
                                kind="ExternalOutput")
    if _DUMPO:
        dbgo_d = nc.dram_tensor("dbgo", [NSLOT * 128, S], BF16,
                                kind="ExternalOutput")

    xln_bounce = nc.dram_tensor("xln_bounce", [D, TOK], BF16)
    xln_gather = nc.dram_tensor("xln_gather", [GROUP * D, TOK], BF16)
    qk_spill = nc.dram_tensor("qk_spill", [16 * 128, S], BF16)
    v_spill = nc.dram_tensor("v_spill", [S, 1024], BF16)
    o_bounce = nc.dram_tensor("o_bounce", [NSLOT * 128, S], BF16)
    o_gather = nc.dram_tensor("o_gather", [GROUP * NSLOT * 128, S], BF16)
    r_spill = nc.dram_tensor("r_spill", [D, TOK], BF16)
    upbf_d = nc.dram_tensor("upbf", [16 * 128, 2048], BF16)
    dnbf_d = nc.dram_tensor("dnbf", [16 * 128, 2048], BF16)
    wqbf_d = nc.dram_tensor("wqbf", [16 * 128, 4096], BF16)
    wvbf_d = nc.dram_tensor("wvbf", [4 * 128, 8192], BF16)

    sem_names = ("PE", "ACT", "DVE", "DW", "DS", "DA", "CC1", "CC2")

    from contextlib import ExitStack
    stack = ExitStack()
    sems = {n: stack.enter_context(nc.semaphore("s" + n)) for n in sem_names}
    em = Em(sems)

    Rh = [None]
    mask_sb = stack.enter_context(nc.sbuf_tensor("mask_sb", [128, 128], F32))
    ones_bf = stack.enter_context(nc.sbuf_tensor("ones_bf", [128, 128], BF16))
    onec_bf = stack.enter_context(nc.sbuf_tensor("onec_bf", [128, 1], BF16))
    ones1_sb = stack.enter_context(nc.sbuf_tensor("ones1_sb", [1, 128], F32))
    bqk_sb = stack.enter_context(nc.sbuf_tensor("bqk_sb", [128, 16], F32))
    bout_sb = stack.enter_context(nc.sbuf_tensor("bout_sb", [128, 32], F32))
    btab_sb = stack.enter_context(nc.sbuf_tensor("btab_sb", [128, nbk], F32))

    def run_phase(name, fn):
        snap = em.snap()
        em.engine, em.eng = "SIM", None
        fn(em)
        with nc.Block(name) as blk:
            for engname, reg in (("PE", blk.tensor), ("ACT", blk.scalar),
                                 ("DVE", blk.vector), ("GP", blk.gpsimd),
                                 ("SY", blk.sync)):
                def body(eng, engname=engname):
                    em.restore(snap)
                    em.engine, em.eng = engname, eng
                    fn(em)
                reg(body)
        em.restore(snap)
        em.engine, em.eng = "SIM", None
        fn(em)

    def p0(em):
        xap = x_d[:, :].rearrange("(c p) t -> p c t", p=128)
        rap = Rh[0][:, :].rearrange("p (c t) -> p c t", c=NCH)
        for q in range(8):
            em.step("GP", lambda g, q=q: g.dma_start(
                out=rap[:, q * 4:(q + 1) * 4, :],
                in_=xap[:, q * 4:(q + 1) * 4, :]),
                name=f"ld_x{q}")
        em.step("SY", lambda s: s.dma_start(out=mask_sb[:, :], in_=mask_d[:, :]))
        em.step("GP", lambda g: g.dma_start(out=ones_bf[:, :], in_=onesc_d[:, :]))
        em.step("GP", lambda g: g.dma_start(out=onec_bf[:, :], in_=onecol_d[:, :]))
        em.step("SY", lambda s: s.dma_start(out=ones1_sb[:, :], in_=ones1_d[:, :]))
        em.step("SY", lambda s: s.dma_start(out=bqk_sb[:, :], in_=bqk_d[:, :]))
        em.step("SY", lambda s: s.dma_start(out=bout_sb[:, :], in_=bout_d[:, :]))
        em.step("SY", lambda s: s.dma_start(out=btab_sb[:, :], in_=btab_d[:, :]),
                name="ld_consts")
        em.step("GP", lambda g: g.wait_ge(em.sems["DW"], em.c["DW"]) and None,
                inc=0)
        em.step("SY", lambda s: s.wait_ge(em.sems["DS"], em.c["DS"]) and None,
                inc=0)
        em.step("ACT", lambda a: a.wait_ge(em.sems["DA"], em.c["DA"]) and None,
                inc=0)

    def ln_emit(em, tag, t, apply_only_stats=True):
        (sq_sb, row_mean, row_ms, row_rstd, row_mrs,
         sum_ps, sq_ps, bc_ps, rstdb_sb, mrsb_sb) = t
        for c in range(NCH):
            em.step("DVE", lambda v, c=c: v.tensor_tensor(
                sq_sb[:, (c % 2) * TOK:(c % 2 + 1) * TOK],
                Rh[0][:, c * TOK:(c + 1) * TOK],
                Rh[0][:, c * TOK:(c + 1) * TOK], ALU.mult),
                name=f"{tag}_sq{c}",
                waits=(f"{tag}_st{c-2}",) if c >= 2 else ())
            def mm(tt, c=c):
                for h in range(2):
                    tt.matmul(sum_ps[0:1, h * 512:(h + 1) * 512],
                              onec_bf[:, :],
                              Rh[0][:, c * TOK + h * 512: c * TOK + (h + 1) * 512],
                              start=(c == 0), stop=(c == NCH - 1))
                last = None
                for h in range(2):
                    last = tt.matmul(
                        sq_ps[0:1, h * 512:(h + 1) * 512], onec_bf[:, :],
                        sq_sb[:, (c % 2) * TOK + h * 512:
                              (c % 2) * TOK + (h + 1) * 512],
                        start=(c == 0), stop=(c == NCH - 1))
                return last
            em.step("PE", mm, name=f"{tag}_st{c}", waits=(f"{tag}_sq{c}",))
        em.step("DVE", lambda v: v.tensor_scalar_mul(
            row_mean[:, :], sum_ps[0:1, :], 1.0 / D),
            name=f"{tag}_mean", waits=(f"{tag}_st{NCH-1}",))
        em.step("DVE", lambda v: v.tensor_scalar_mul(
            row_ms[:, :], sq_ps[0:1, :], 1.0 / D), name=f"{tag}_ms")
        em.step("ACT", lambda a: a.activation(
            row_rstd[:, :], row_mean[:, :], AF.Square),
            name=f"{tag}_m2", waits=(f"{tag}_mean",))
        em.step("DVE", lambda v: v.tensor_tensor(
            row_ms[:, :], row_ms[:, :], row_rstd[:, :], ALU.subtract),
            name=f"{tag}_var0", waits=(f"{tag}_m2",), own_waits=(f"{tag}_ms",))
        em.step("DVE", lambda v: v.tensor_scalar_add(
            row_ms[:, :], row_ms[:, :], EPS),
            name=f"{tag}_var", own_waits=(f"{tag}_var0",))
        em.step("ACT", lambda a: a.activation(
            row_rstd[:, :], row_ms[:, :], AF.Ln),
            name=f"{tag}_ln", waits=(f"{tag}_var",))
        em.step("ACT", lambda a: a.activation(
            row_rstd[:, :], row_rstd[:, :], AF.Exp, scale=-0.5),
            name=f"{tag}_rstd", own_waits=(f"{tag}_ln",))
        em.step("DVE", lambda v: v.tensor_tensor(
            row_mrs[:, :], row_mean[:, :], row_rstd[:, :], ALU.mult),
            name=f"{tag}_mrs", waits=(f"{tag}_rstd",),
            own_waits=(f"{tag}_mean",))
        def bc(tt):
            last = None
            for h in range(2):
                tt.matmul(bc_ps[:, h * 512:(h + 1) * 512], ones1_sb[:, :],
                          row_rstd[0:1, h * 512:(h + 1) * 512],
                          start=True, stop=True)
                last = tt.matmul(bc_ps[:, 1024 + h * 512:1024 + (h + 1) * 512],
                                 ones1_sb[:, :],
                                 row_mrs[0:1, h * 512:(h + 1) * 512],
                                 start=True, stop=True)
            return last
        em.step("PE", bc, name=f"{tag}_bc", waits=(f"{tag}_mrs",))
        em.step("DVE", lambda v: v.tensor_copy(rstdb_sb[:, :], bc_ps[:, 0:1024]),
                name=f"{tag}_rb", waits=(f"{tag}_bc",))
        em.step("DVE", lambda v: v.tensor_copy(mrsb_sb[:, :], bc_ps[:, 1024:2048]),
                name=f"{tag}_mb")

    def ln_apply(em, tag, c, rstdb_sb, mrsb_sb, tscratch, dst, dsl,
                 extra_waits=()):
        em.step("DVE", lambda v: v.tensor_tensor(
            tscratch[:, :], Rh[0][:, c * TOK:(c + 1) * TOK], rstdb_sb[:, :],
            ALU.mult), name=f"{tag}_xa{c}",
            waits=(f"{tag.split('_')[0]}_rb", f"{tag.split('_')[0]}_mb")
            + tuple(extra_waits))
        em.step("DVE", lambda v: v.tensor_tensor(
            dst[:, dsl], tscratch[:, :], mrsb_sb[:, :], ALU.subtract),
            name=f"{tag}_xln{c}", own_waits=(f"{tag}_xa{c}",))

    NWB = 5

    def ffn_phase(em, tag, t, final):
        (lnt, up_sb, dn_sb, dnf_sb, hid_sb, xln_scr, lnsc, t05, out_stg,
         up_ps, dn_ps) = t
        ln_emit(em, tag, lnt)

        def dn_block(hf):
            for oc2 in range(2):
                def mmdn(tt, hf=hf, oc2=oc2):
                    last = None
                    for ic2 in range(8):
                        for h in range(2):
                            last = tt.matmul(
                                dn_ps[oc2 % 2][:, h * 512:(h + 1) * 512],
                                dn_sb[hf % NWB][:, ic2 * 256 + oc2 * 128:
                                              ic2 * 256 + oc2 * 128 + 128],
                                hid_sb[hf % 2][:, ic2 * TOK + h * 512:
                                       ic2 * TOK + (h + 1) * 512],
                                start=(ic2 == 0), stop=(ic2 == 7))
                    return last
                em.step("PE", mmdn, name=f"{tag}_dn{hf}_{oc2}",
                        waits=(f"{tag}_wdn{hf}", f"{tag}_gelu{hf}_7")
                        + ((f"{tag}_acc{hf}_{oc2-1}",) if oc2 >= 1 else
                           ((f"{tag}_acc{hf-1}_1",) if hf >= 1 else ())))
                em.step("DVE", lambda v, hf=hf, oc2=oc2: v.tensor_scalar_mul(
                    t05[oc2 % 2][:, :], dn_ps[oc2 % 2][:, :], 0.5),
                    name=f"{tag}_t05_{hf}_{oc2}",
                    waits=(f"{tag}_dn{hf}_{oc2}",))
                c = 2 * hf + oc2
                if not final:
                    em.step("DVE", lambda v, c=c, oc2=oc2: v.tensor_tensor(
                        Rh[0][:, c * TOK:(c + 1) * TOK],
                        Rh[0][:, c * TOK:(c + 1) * TOK],
                        t05[oc2 % 2][:, :], ALU.add),
                        name=f"{tag}_acc{hf}_{oc2}",
                        waits=(f"{tag}_t05_{hf}_{oc2}",))
                else:
                    em.step("DVE", lambda v, c=c, oc2=oc2: v.tensor_tensor(
                        out_stg[oc2 % 2][:, :], Rh[0][:, c * TOK:(c + 1) * TOK],
                        t05[oc2 % 2][:, :], ALU.add),
                        name=f"{tag}_acc{hf}_{oc2}",
                        waits=(f"{tag}_t05_{hf}_{oc2}",)
                        + ((f"{tag}_yd{hf}_{oc2-1}",) if oc2 >= 1 else
                           ((f"{tag}_yd{hf-1}_1",) if hf >= 1 else ())))
                    em.step("SY", lambda s, c=c, oc2=oc2: s.dma_start(
                        out=y_d[c * 128:(c + 1) * 128, :],
                        in_=out_stg[oc2 % 2][:, :]),
                        name=f"{tag}_yd{hf}_{oc2}",
                        waits=(f"{tag}_acc{hf}_{oc2}",))
            em.mil[f"{tag}_dn_done{hf}"] = em.mil[f"{tag}_dn{hf}_1"]

        for hf in range(FH):
            if not final:
                em.step("GP", lambda g, hf=hf: g.dma_start(
                    out=up_sb[hf % NWB][:, :],
                    in_=up_d[hf * 128:(hf + 1) * 128, :]),
                    name=f"{tag}_wup{hf}",
                    waits=((f"{tag}_up_done{hf-NWB}", f"{tag}_spu{hf-NWB}")
                           if hf >= NWB else ()))
                em.step("SY", lambda s, hf=hf: s.dma_start(
                    out=upbf_d[hf * 128:(hf + 1) * 128, :],
                    in_=up_sb[hf % NWB][:, :]), name=f"{tag}_spu{hf}",
                    waits=(f"{tag}_wup{hf}",))
                em.step("SY", lambda s, hf=hf: s.dma_start(
                    out=dnf_sb[hf % 2][:, :],
                    in_=dn_d[hf * 128:(hf + 1) * 128, :]),
                    name=f"{tag}_wdf{hf}",
                    waits=(f"{tag}_wdn{hf-2}",) if hf >= 2 else ())
                em.step("DVE", lambda v, hf=hf: v.tensor_copy(
                    dn_sb[hf % NWB][:, :], dnf_sb[hf % 2][:, :]),
                    name=f"{tag}_wdn{hf}",
                    waits=(f"{tag}_wdf{hf}",)
                    + ((f"{tag}_dn_done{hf-NWB}", f"{tag}_spd{hf-NWB}")
                       if hf >= NWB else ()))
                em.step("SY", lambda s, hf=hf: s.dma_start(
                    out=dnbf_d[hf * 128:(hf + 1) * 128, :],
                    in_=dn_sb[hf % NWB][:, :]), name=f"{tag}_spd{hf}",
                    waits=(f"{tag}_wdn{hf}",))
            else:
                em.step("GP", lambda g, hf=hf: g.dma_start(
                    out=up_sb[hf % NWB][:, :],
                    in_=upbf_d[hf * 128:(hf + 1) * 128, :]),
                    name=f"{tag}_wup{hf}",
                    waits=(f"{tag}_up_done{hf-NWB}",) if hf >= NWB else ())
                em.step("SY", lambda s, hf=hf: s.dma_start(
                    out=dn_sb[hf % NWB][:, :],
                    in_=dnbf_d[hf * 128:(hf + 1) * 128, :]),
                    name=f"{tag}_wdn{hf}",
                    waits=(f"{tag}_dn_done{hf-NWB}",) if hf >= NWB else ())
            hp = (hf % 2) * 2 * TOK
            if hf == 0:
                for ci in range(2):
                    ln_apply(em, f"{tag}_h0", ci, lnt[-2], lnt[-1],
                             lnsc, xln_scr, slice(ci * TOK, (ci + 1) * TOK))
            for oc in range(8):
                def mmup(tt, hf=hf, oc=oc, hp=hp):
                    last = None
                    for ci in range(2):
                        for h in range(2):
                            last = tt.matmul(
                                up_ps[oc % 2][:, h * 512:(h + 1) * 512],
                                up_sb[hf % NWB][:, ci * 1024 + oc * 128:
                                              ci * 1024 + oc * 128 + 128],
                                xln_scr[:, hp + ci * TOK + h * 512:
                                        hp + ci * TOK + (h + 1) * 512],
                                start=(ci == 0), stop=(ci == 1))
                    return last
                em.step("PE", mmup, name=f"{tag}_up{hf}_{oc}",
                        waits=(f"{tag}_wup{hf}", f"{tag}_h{hf}_xln{2*hf+1}")
                        + ((f"{tag}_gelu{hf}_{oc-2}",) if oc >= 2 else
                           ((f"{tag}_gelu{hf-1}_{oc+6}",) if hf >= 1 else ())))
                em.step("ACT", lambda a, hf=hf, oc=oc: a.activation(
                    hid_sb[hf % 2][:, oc * TOK:(oc + 1) * TOK],
                    up_ps[oc % 2][:, :], AF.Gelu),
                    name=f"{tag}_gelu{hf}_{oc}", waits=(f"{tag}_up{hf}_{oc}",)
                    + ((f"{tag}_dn{hf-2}_1",) if hf >= 2 else ()))
            em.mil[f"{tag}_up_done{hf}"] = em.mil[f"{tag}_up{hf}_7"]
            if hf + 1 < FH:
                hp1 = ((hf + 1) % 2) * 2 * TOK
                for ci in range(2):
                    c = 2 * (hf + 1) + ci
                    ln_apply(em, f"{tag}_h{hf+1}", c, lnt[-2], lnt[-1],
                             lnsc, xln_scr, slice(hp1 + ci * TOK,
                                                  hp1 + (ci + 1) * TOK),
                             extra_waits=((f"{tag}_up{hf-1}_7",)
                                          if hf >= 1 and ci == 0 else ()))
            if hf >= 1:
                dn_block(hf - 1)
        dn_block(FH - 1)
        em.step("SY", lambda s: s.wait_ge(em.sems["DS"], em.c["DS"]) and
                None, inc=0)
        em.step("GP", lambda g: g.wait_ge(em.sems["DW"], em.c["DW"]) and None,
                inc=0)
        em.step("ACT", lambda a: a.wait_ge(em.sems["DA"], em.c["DA"]) and None,
                inc=0)

    def make_ln_tensors(st, pfx, big, big2):
        sq_sb = st.enter_context(nc.sbuf_tensor(pfx + "sq", [128, 2 * TOK], BF16))
        rows = [st.enter_context(nc.sbuf_tensor(pfx + n, [1, TOK], F32))
                for n in ("mean", "ms", "rstd", "mrs")]
        sum_ps = big2[0:1, 0:TOK]
        sq_ps = big2[0:1, TOK:2 * TOK]
        bc_ps = big[:, 0:2 * TOK]
        rstdb = st.enter_context(nc.sbuf_tensor(pfx + "rstdb", [128, TOK], F32))
        mrsb = st.enter_context(nc.sbuf_tensor(pfx + "mrsb", [128, TOK], F32))
        return (sq_sb, *rows, sum_ps, sq_ps, bc_ps, rstdb, mrsb)

    def make_ffn_tensors(st, pfx, stage_dn):
        big = st.enter_context(nc.psum_tensor(pfx + "bigA", [128, 2 * TOK], F32))
        big2 = st.enter_context(nc.psum_tensor(pfx + "bigB", [128, 2 * TOK], F32))
        lnt = make_ln_tensors(st, pfx + "ln", big, big2)
        up_sb = [st.enter_context(nc.sbuf_tensor(f"{pfx}up{i}", [128, 2048], BF16))
                 for i in range(NWB)]
        dn_sb = [st.enter_context(nc.sbuf_tensor(f"{pfx}dn{i}", [128, 2048], BF16))
                 for i in range(NWB)]
        dnf_sb = ([st.enter_context(
            nc.sbuf_tensor(f"{pfx}dnf{i}", [128, 2048], F32))
            for i in range(2)] if stage_dn else None)
        hid_sb = [st.enter_context(
            nc.sbuf_tensor(f"{pfx}hid{i}", [128, 8 * TOK], BF16))
            for i in range(2)]
        xln_scr = st.enter_context(nc.sbuf_tensor(pfx + "xln", [128, 4 * TOK], BF16))
        lnsc = st.enter_context(nc.sbuf_tensor(pfx + "lnsc", [128, TOK], F32))
        t05 = [st.enter_context(nc.sbuf_tensor(f"{pfx}t05{i}", [128, TOK], F32))
               for i in range(2)]
        out_stg = ([st.enter_context(
            nc.sbuf_tensor(f"{pfx}os{i}", [128, TOK], F32))
            for i in range(2)] if not stage_dn else None)
        up_ps = [big[:, i * TOK:(i + 1) * TOK] for i in range(2)]
        dn_ps = [big2[:, i * TOK:(i + 1) * TOK] for i in range(2)]
        return (lnt, up_sb, dn_sb, dnf_sb, hid_sb, xln_scr, lnsc, t05, out_stg,
                up_ps, dn_ps)

    def p2a_make(st):
        big = st.enter_context(nc.psum_tensor("p2bigA", [128, 2 * TOK], F32))
        big2 = st.enter_context(nc.psum_tensor("p2bigB", [128, 2 * TOK], F32))
        lnt = make_ln_tensors(st, "l2", big, big2)
        xchunk = [st.enter_context(nc.sbuf_tensor(f"xc{i}", [128, TOK], BF16))
                  for i in range(2)]
        tsc = st.enter_context(nc.sbuf_tensor("l2t", [128, TOK], F32))
        return (lnt, xchunk, tsc)

    def p2b_make(st):
        big = st.enter_context(nc.psum_tensor("p2bigC", [128, 2 * TOK], F32))
        big2 = st.enter_context(nc.psum_tensor("p2bigD", [128, 2 * TOK], F32))
        xg = [st.enter_context(nc.sbuf_tensor(f"xg{i}", [128, NCH * TOK], BF16))
              for i in range(2)]
        wq_sb = [st.enter_context(nc.sbuf_tensor(f"wq{i}", [128, 4096], BF16))
                 for i in range(3)]
        wv_sb = [st.enter_context(nc.sbuf_tensor(f"wv{i}", [128, 8192], BF16))
                 for i in range(2)]
        bv_sb = st.enter_context(nc.sbuf_tensor("bv", [128, 1024], F32))
        qk_stg = [st.enter_context(nc.sbuf_tensor(f"qs{i}", [128, TOK], BF16))
                  for i in range(2)]
        v_stg = [st.enter_context(nc.sbuf_tensor(f"vs{i}", [128, 256], BF16))
                 for i in range(2)]
        qk_ps = [big[:, i * TOK:(i + 1) * TOK] for i in range(2)]
        v_ps = [big2[:, i * 512:i * 512 + 256] for i in range(2)]
        return (xg, wq_sb, wv_sb, bv_sb, qk_stg, v_stg, qk_ps, v_ps)

    def p2a(em, t):
        (lnt, xchunk, tsc) = t
        ln_emit(em, "l2", lnt)
        for c in range(NCH):
            ln_apply(em, f"l2_c", c, lnt[-2], lnt[-1], tsc,
                     xchunk[c % 2], slice(0, TOK),
                     extra_waits=(f"l2_sp{c-2}",) if c >= 2 else ())
            em.step("ACT", lambda a, c=c: a.dma_start(
                out=xln_bounce[c * 128:(c + 1) * 128, :],
                in_=xchunk[c % 2][:, :]), name=f"l2_sp{c}", sem="DA",
                waits=(f"l2_c_xln{c}",))
        em.step("GP", lambda g: g.dma_start(
            out=r_spill[:, :].rearrange("(c p) t -> p c t", p=128),
            in_=Rh[0][:, :].rearrange("p (c t) -> p c t", c=NCH)),
            name="r_spill")
        def cc(g, sem):
            g.wait_ge(em.sems[em.mil[f"l2_sp{NCH-1}"][0]],
                      em.mil[f"l2_sp{NCH-1}"][1])
            g.collective_compute(
                "AllGather", ALU.bypass,
                replica_groups=[[0, 1, 2, 3], [4, 5, 6, 7]],
                ins=[xln_bounce[:, :].opt()],
                outs=[xln_gather[:, :].opt()]).then_inc(sem, 1)
        em.raw("GP", cc, name="cc1", inc=1, sem="CC1")

    def p2b(em, t):
        (xg, wq_sb, wv_sb, bv_sb, qk_stg, v_stg, qk_ps, v_ps) = t
        em.step("SY", lambda s: s.dma_start(out=bv_sb[:, :], in_=bv_d[:, :]),
                name="ld_bv")
        gap = xln_gather[:, :].rearrange("(tt c p) t -> tt p c t",
                                         tt=GROUP, p=128)
        xgv = [x[:, :].rearrange("p (c t) -> p c t", c=NCH) for x in xg]
        for tt in range(GROUP):
            em.step("SY", lambda s, tt=tt: s.dma_start(
                out=xgv[tt % 2], in_=gap[tt, :, :, :]), name=f"xg{tt}",
                waits=("cc1",)
                + ((f"p2_vdone{tt-2}",) if tt >= 2 and _P2V else
                   ((f"qkmm{tt*16-17}",) if tt >= 2 and
                    f"qkmm{tt*16-17}" in em.mil else ())))
            for oc in range(16):
                i = tt * 16 + oc
                if i >= _P2QK:
                    continue
                if tt == 0:
                    em.step("GP", lambda g, oc=oc, i=i: g.dma_start(
                        out=wq_sb[i % 3][:, :],
                        in_=w1qk_d[oc * 128:(oc + 1) * 128, :]),
                        name=f"wq{i}",
                        waits=((f"qkmm{i-3}", f"wqsp{i-3}")
                               if i >= 3 else ()))
                    em.step("SY", lambda s, oc=oc, i=i: s.dma_start(
                        out=wqbf_d[oc * 128:(oc + 1) * 128, :],
                        in_=wq_sb[i % 3][:, :]), name=f"wqsp{oc}",
                        waits=(f"wq{i}",))
                def mmqk(tt_, i=i, oc=oc):
                    last = None
                    for ic in range(NCH):
                        for h in range(2):
                            last = tt_.matmul(
                                qk_ps[i % 2][:, h * 512:(h + 1) * 512],
                                wq_sb[i % 3][:, ic * 128:(ic + 1) * 128],
                                xg[tt % 2][:, ic * TOK + h * 512:
                                           ic * TOK + (h + 1) * 512],
                                start=(ic == 0), stop=(ic == NCH - 1))
                    return last
                em.step("PE", mmqk, name=f"qkmm{i}",
                        waits=(f"xg{tt}", f"wq{i}")
                        + ((f"qkev{i-2}",) if i >= 2 else ()))
                em.step("DVE", lambda v, i=i, oc=oc: v.tensor_scalar_add(
                    qk_stg[i % 2][:, :], qk_ps[i % 2][:, :],
                    bqk_sb[:, oc:oc + 1]), name=f"qkev{i}",
                    waits=(f"qkmm{i}",)
                    + ((f"qksp{i-2}",) if i >= 2 else ()))
                em.step("ACT", lambda a, i=i, oc=oc, tt=tt: a.dma_start(
                    out=qk_spill[oc * 128:(oc + 1) * 128,
                                 tt * TOK:(tt + 1) * TOK],
                    in_=qk_stg[i % 2][:, :]), name=f"qksp{i}", sem="DA",
                    waits=(f"qkev{i}",))
                if 16 <= i + 3 < 64:
                    rw = ((f"wqsp{(i+3) % 16}", f"wqsp{i}") if i < 16
                          else ("wqsp15",))
                    em.step("ACT", lambda a, i=i: a.dma_start(
                        out=wq_sb[(i + 3) % 3][:, :],
                        in_=wqbf_d[((i + 3) % 16) * 128:
                                   ((i + 3) % 16 + 1) * 128, :]),
                        name=f"wq{i+3}", sem="DA",
                        waits=(f"qkmm{i}",) + rw)
            for vg in range(4 if _P2V else 0):
                gi = tt * 4 + vg
                if tt == 0:
                    em.step("GP", lambda g, vg=vg, gi=gi: g.dma_start(
                        out=wv_sb[gi % 2][:, :],
                        in_=w1v_d[vg * 128:(vg + 1) * 128, :]),
                        name=f"wv{gi}",
                        waits=((f"vmm{gi-2}_7", f"wvsp{gi-2}")
                               if gi >= 2 else ()))
                    em.step("SY", lambda s, vg=vg, gi=gi: s.dma_start(
                        out=wvbf_d[vg * 128:(vg + 1) * 128, :],
                        in_=wv_sb[gi % 2][:, :]), name=f"wvsp{vg}",
                        waits=(f"wv{gi}",))
                for ttile in range(8):
                    vi = gi * 8 + ttile
                    def mmv(tt_, tt=tt, ttile=ttile, vi=vi, gi=gi):
                        last = None
                        for ic in range(NCH):
                            last = tt_.matmul(
                                v_ps[ttile % 2][:, :],
                                xg[tt % 2][:, ic * TOK + ttile * 128:
                                           ic * TOK + ttile * 128 + 128],
                                wv_sb[gi % 2][:, ic * 256:(ic + 1) * 256],
                                start=(ic == 0), stop=(ic == NCH - 1))
                        return last
                    em.step("PE", mmv, name=f"vmm{gi}_{ttile}",
                            waits=(f"wv{gi}", f"xg{tt}")
                            + ((f"vev{vi-2}",) if vi >= 2 else ()))
                    em.step("DVE", lambda v, vi=vi, vg=vg, ttile=ttile:
                            v.tensor_tensor(
                                v_stg[ttile % 2][:, :], v_ps[ttile % 2][:, :],
                                bv_sb[:, vg * 256:(vg + 1) * 256], ALU.add),
                            name=f"vev{vi}",
                            waits=(f"vmm{gi}_{ttile}", "ld_bv")
                            + ((f"vsp{vi-2}",) if vi >= 2 else ()))
                    em.step("ACT", lambda a, tt=tt, ttile=ttile, vg=vg, vi=vi:
                            a.dma_start(
                                out=v_spill[tt * TOK + ttile * 128:
                                            tt * TOK + (ttile + 1) * 128,
                                            vg * 256:(vg + 1) * 256],
                                in_=v_stg[ttile % 2][:, :]),
                            name=f"vsp{vi}", sem="DA", waits=(f"vev{vi}",))
                if ttile == 7 and 4 <= gi + 2 < 16:
                    rw = ((f"wvsp{(gi+2) % 4}", f"wvsp{gi}") if gi < 4
                          else ("wvsp3",))
                    em.step("ACT", lambda a, gi=gi: a.dma_start(
                        out=wv_sb[gi % 2][:, :],
                        in_=wvbf_d[((gi + 2) % 4) * 128:
                                   ((gi + 2) % 4 + 1) * 128, :]),
                        name=f"wv{gi+2}", sem="DA",
                        waits=(f"vmm{gi}_7",) + rw)
            if _P2V:
                em.mil[f"p2_vdone{tt}"] = em.mil[f"vmm{tt*4+3}_7"]
        em.step("GP", lambda g: g.wait_ge(em.sems["DW"], em.c["DW"]) and None,
                inc=0)
        em.step("SY", lambda s: s.wait_ge(em.sems["DS"], em.c["DS"]) and None,
                inc=0)
        em.step("ACT", lambda a: a.wait_ge(em.sems["DA"], em.c["DA"]) and None,
                inc=0)

    NB = 4
    DEPTH = 3

    def p3_make(st):
        q_sb = [st.enter_context(nc.sbuf_tensor(f"q{i}", [128, S], BF16))
                for i in range(2)]
        k_sb = [st.enter_context(nc.sbuf_tensor(f"k{i}", [128, S], BF16))
                for i in range(2)]
        v_sb = st.enter_context(nc.sbuf_tensor("vall", [128, 32 * 1024], BF16))
        e_sb = st.enter_context(nc.sbuf_tensor("e_sb", [128, 8 * 512], F32))
        pbar = [st.enter_context(nc.sbuf_tensor(f"pb{i}", [128, 512], BF16))
                for i in range(NB)]
        nsc = [st.enter_context(nc.sbuf_tensor(f"nsc{i}", [128, 512], F32))
               for i in range(4)]
        o_stg = [st.enter_context(nc.sbuf_tensor(f"og{i}", [128, 512], BF16))
                 for i in range(2)]
        s_ps = [st.enter_context(nc.psum_tensor(f"sps{i}", [128, 512], F32))
                for i in range(NB)]
        o_ps = [st.enter_context(nc.psum_tensor(f"ops{i}", [128, 512], F32))
                for i in range(2)]
        l_ps = [st.enter_context(nc.psum_tensor(f"lps{i}", [128, 512], F32))
                for i in range(2)]
        return (q_sb, k_sb, v_sb, e_sb, pbar, nsc, o_stg, s_ps, o_ps, l_ps)

    def p3(em, t):
        (q_sb, k_sb, v_sb, e_sb, pbar, nsc, o_stg, s_ps, o_ps, l_ps) = t
        em.step("SY", lambda s: s.dma_start(out=e_sb[:, :], in_=etab_d[:, :]),
                name="ld_e")
        vap = v_spill[:, :].rearrange("(kt p) o -> p kt o", p=128)
        vsv = v_sb[:, :].rearrange("p (kt o) -> p kt o", o=1024)
        for vc in range(4):
            em.step("GP" if vc % 2 == 0 else "SY",
                    lambda g, vc=vc: g.dma_start(
                        out=vsv[:, vc * 8:(vc + 1) * 8, :],
                        in_=vap[:, vc * 8:(vc + 1) * 8, :]),
                    name=f"a_v{vc}")
        for jj in range(2):
            em.step("SY", lambda s, jj=jj: s.dma_start(
                out=q_sb[jj][:, :],
                in_=qk_spill[jj * 128:(jj + 1) * 128, :]), name=f"a_q{jj}")
            em.step("SY", lambda s, jj=jj: s.dma_start(
                out=k_sb[jj][:, :],
                in_=qk_spill[(8 + jj) * 128:(9 + jj) * 128, :]),
                name=f"a_k{jj}")
        flat = []
        for j in range(NSLOT):
            for qc in range(8):
                items = sched[j][qc]
                for it, (kt, qlo, sl) in enumerate(items):
                    flat.append((j, qc, it, len(items), kt, qlo, sl))
        nflat = len(flat)

        def emit_st(g):
            j, qc, it, nk, kt, qlo, sl = flat[g]
            qhi = sl[-1][1]
            if qc == 0 and it == 0 and 2 <= j + 1 < NSLOT:
                jn = j + 1
                em.step("SY", lambda s, jn=jn: s.dma_start(
                    out=q_sb[jn % 2][:, :],
                    in_=qk_spill[jn * 128:(jn + 1) * 128, :]),
                    name=f"a_q{jn}", waits=(f"a_st_last{jn-2}",))
                em.step("SY", lambda s, jn=jn: s.dma_start(
                    out=k_sb[jn % 2][:, :],
                    in_=qk_spill[(8 + jn) * 128:(9 + jn) * 128, :]),
                    name=f"a_k{jn}", waits=(f"a_st_last{jn-2}",))
            def mmst(tt, j=j, qc=qc, kt=kt, qlo=qlo, qhi=qhi, g=g):
                return tt.matmul(
                    s_ps[g % NB][:, qlo:qhi],
                    k_sb[j % 2][:, kt * 128:(kt + 1) * 128],
                    q_sb[j % 2][:, qc * 512 + qlo:qc * 512 + qhi],
                    start=True, stop=True)
            em.step("PE", mmst, name=f"a_stG{g}",
                    waits=(f"a_q{j}", f"a_k{j}")
                    + ((f"a_expG{g-NB}",) if g >= NB else ()))
            if g == flat_last_of_slot[j]:
                em.mil[f"a_st_last{j}"] = em.mil[f"a_stG{g}"]
            diag = kt >= 4 * qc
            if diag:
                dq = 128 * (kt - 4 * qc)
                em.step("DVE", lambda v, g=g, dq=dq: v.tensor_tensor(
                    s_ps[g % NB][:, dq:dq + 128],
                    s_ps[g % NB][:, dq:dq + 128],
                    mask_sb[:, :], ALU.add),
                    name=f"a_maskG{g}", waits=(f"a_stG{g}",))
            def expf(a, j=j, sl=sl, g=g):
                last = None
                for (aa, bb, dk) in sl:
                    bi = bkeys[(j, dk)]
                    last = a.activation(
                        pbar[g % NB][:, aa:bb], s_ps[g % NB][:, aa:bb],
                        AF.Exp, bias=btab_sb[:, bi:bi + 1],
                        scale=SCALE)
                return last
            em.step("ACT", expf, name=f"a_expG{g}",
                    waits=((f"a_maskG{g}",) if diag else (f"a_stG{g}",))
                    + ("ld_consts",)
                    + ((f"a_avG{g-NB}",) if g >= NB else ()))

        def emit_av(g):
            j, qc, it, nk, kt, qlo, sl = flat[g]
            qhi = sl[-1][1]
            first = (it == 0)
            last_it = (it == nk - 1)
            def mmav(tt, j=j, qc=qc, kt=kt, qlo=qlo, qhi=qhi, g=g,
                     first=first, last_it=last_it):
                tt.matmul(o_ps[qc % 2][:, qlo:qhi],
                          v_sb[:, kt * 1024 + j * 128:
                               kt * 1024 + (j + 1) * 128],
                          pbar[g % NB][:, qlo:qhi],
                          start=first, stop=last_it)
                return tt.matmul(l_ps[qc % 2][:, qlo:qhi],
                                 ones_bf[:, :],
                                 pbar[g % NB][:, qlo:qhi],
                                 start=first, stop=last_it)
            em.step("PE", mmav, name=f"a_avG{g}",
                    waits=(f"a_expG{g}", f"a_v{kt//8}")
                    + ((f"a_norm_done{j}_{qc-2}",) if first and qc >= 2
                       else ())
                    + ((f"a_normprev{j}_{qc}",) if first and
                       f"a_normprev{j}_{qc}" in em.mil else ()))
            if not last_it:
                return
            war_le = ((f"a_rp{j}_{qc-2}",) if qc >= 2 else
                      ((f"a_rp{j-1}_{qc+6}",) if j >= 1 else ()))
            war_rp = ((f"a_on{j}_{qc-2}",) if qc >= 2 else
                      ((f"a_on{j-1}_{qc+6}",) if j >= 1 else ()))
            em.step("DVE", lambda v, j=j, qc=qc: v.tensor_tensor(
                nsc[(qc % 2) * 2][:, :], l_ps[qc % 2][:, :],
                e_sb[:, j * 512:(j + 1) * 512], ALU.mult),
                name=f"a_le{j}_{qc}",
                waits=(f"a_avG{g}", "ld_e") + war_le)
            em.step("ACT", lambda a, j=j, qc=qc: a.activation(
                nsc[(qc % 2) * 2][:, :], nsc[(qc % 2) * 2][:, :], AF.Ln),
                name=f"a_ln{j}_{qc}", waits=(f"a_le{j}_{qc}",))
            em.step("ACT", lambda a, j=j, qc=qc: a.activation(
                nsc[(qc % 2) * 2 + 1][:, :], nsc[(qc % 2) * 2][:, :],
                AF.Exp, scale=-1.0), name=f"a_rp{j}_{qc}",
                own_waits=(f"a_ln{j}_{qc}",), waits=war_rp)
            em.step("DVE", lambda v, j=j, qc=qc: v.tensor_tensor(
                nsc[(qc % 2) * 2 + 1][:, :], nsc[(qc % 2) * 2 + 1][:, :],
                e_sb[:, j * 512:(j + 1) * 512], ALU.mult),
                name=f"a_r{j}_{qc}", waits=(f"a_rp{j}_{qc}",))
            em.step("DVE", lambda v, j=j, qc=qc: v.tensor_tensor(
                o_stg[qc % 2][:, :], o_ps[qc % 2][:, :],
                nsc[(qc % 2) * 2 + 1][:, :], ALU.mult),
                name=f"a_on{j}_{qc}", own_waits=(f"a_r{j}_{qc}",),
                waits=((f"a_od{j}_{qc-2}",) if qc >= 2 else
                       ((f"a_od{j-1}_{qc+6}",) if j >= 1 else ())))
            em.mil[f"a_norm_done{j}_{qc}"] = em.mil[f"a_on{j}_{qc}"]
            if qc == 6:
                em.mil[f"a_normprev{j+1}_0"] = em.mil[f"a_norm_done{j}_6"]
            if qc == 7:
                em.mil[f"a_normprev{j+1}_1"] = em.mil[f"a_norm_done{j}_7"]
            em.step("SY", lambda s, j=j, qc=qc: s.dma_start(
                out=o_bounce[j * 128:(j + 1) * 128,
                             qc * 512:(qc + 1) * 512],
                in_=o_stg[qc % 2][:, :]), name=f"a_od{j}_{qc}",
                waits=(f"a_on{j}_{qc}",))
            if qc == 7:
                def ccj(g, sem, j=j):
                    g.wait_ge(em.sems["DS"], em.mil[f"a_od{j}_7"][1])
                    g.collective_compute(
                        "AllGather", ALU.bypass,
                        replica_groups=[[0, 1, 2, 3], [4, 5, 6, 7]],
                        ins=[o_bounce[j * 128:(j + 1) * 128, :].opt()],
                        outs=[o_gather[j * 512:(j + 1) * 512, :].opt()]
                    ).then_inc(sem, 1)
                em.raw("GP", ccj, name=f"cc2_{j}", inc=1, sem="CC2")

        flat_last_of_slot = {}
        for g, tile in enumerate(flat):
            flat_last_of_slot[tile[0]] = g
        for pos in range(nflat + DEPTH):
            if pos < nflat:
                emit_st(pos)
            if pos >= DEPTH:
                emit_av(pos - DEPTH)
        em.step("GP", lambda g: g.wait_ge(em.sems["DW"], em.c["DW"]) and None,
                inc=0)
        em.step("SY", lambda s: s.wait_ge(em.sems["DS"], em.c["DS"]) and None,
                inc=0)

    def p4_make(st):
        o_own = st.enter_context(nc.sbuf_tensor("oown", [128, NCH * TOK], BF16))
        w2_sb = [st.enter_context(nc.sbuf_tensor(f"w2_{i}", [128, 4096], BF16))
                 for i in range(3)]
        w2f = [st.enter_context(nc.sbuf_tensor(f"w2f{i}", [128, 3072], F32))
               for i in range(2)]
        t05 = [st.enter_context(nc.sbuf_tensor(f"p4t{i}", [128, TOK], F32))
               for i in range(2)]
        op_ps = [st.enter_context(nc.psum_tensor(f"opp{i}", [128, TOK], F32))
                 for i in range(2)]
        return (o_own, w2_sb, w2f, t05, op_ps)

    def p4(em, t):
        (o_own, w2_sb, w2f, t05, op_ps) = t
        em.step("SY", lambda s: s.dma_start(
            out=Rh[0][:, :].rearrange("p (c t) -> p c t", c=NCH),
            in_=r_spill[:, :].rearrange("(c p) t -> p c t", p=128)),
            name="p4_rld", waits=("r_spill",))
        def ldo(g, sem):
            pid = nc.partition_id([ET.Pool])
            reg = g.compute_val(pid)
            for jj in range(NSLOT):
                g.wait_ge(em.sems["CC2"], jj + 1)
                for i in g.Switch(reg, 8):
                    rr = i % 4
                    for q4 in range(4):
                        ic = jj * 4 + q4
                        g.dma_start(
                            out=o_own[:, ic * TOK:(ic + 1) * TOK],
                            in_=o_gather[ic * 128:(ic + 1) * 128,
                                         rr * TOK:(rr + 1) * TOK]
                        ).then_inc(sem, 16)
        def emit_wa(oc):
            em.step("GP", lambda g, oc=oc: g.dma_start(
                out=w2_sb[oc % 3][:, 0:1024],
                in_=w2_d[oc * 128:(oc + 1) * 128, 0:1024]), name=f"p4_wa{oc}",
                waits=(f"p4_mm{oc-3}",) if oc >= 3 else ())

        def emit_wf(oc):
            em.step("SY", lambda s, oc=oc: s.dma_start(
                out=w2f[oc % 2][:, :],
                in_=w2_d[oc * 128:(oc + 1) * 128, 1024:4096]),
                name=f"p4_wf{oc}",
                waits=(f"p4_cv{oc-2}",) if oc >= 2 else ())

        def emit_cv(oc):
            em.step("DVE", lambda v, oc=oc: v.tensor_copy(
                w2_sb[oc % 3][:, 1024:4096], w2f[oc % 2][:, :]),
                name=f"p4_cv{oc}",
                waits=(f"p4_wf{oc}",)
                + ((f"p4_mm{oc-3}",) if oc >= 3 else ()))

        emit_wa(0)
        emit_wa(1)
        em.raw("GP", ldo, name="p4_oo", inc=16 * NCH, sem="DW")
        emit_wf(0)
        emit_wf(1)
        emit_cv(0)
        for oc in range(NCH):
            if oc + 2 < NCH:
                emit_wa(oc + 2)
                emit_wf(oc + 2)
            if oc + 1 < NCH:
                emit_cv(oc + 1)
            def mmo(tt, oc=oc):
                last = None
                for ic in range(NCH):
                    for h in range(2):
                        last = tt.matmul(
                            op_ps[oc % 2][:, h * 512:(h + 1) * 512],
                            w2_sb[oc % 3][:, ic * 128:(ic + 1) * 128],
                            o_own[:, ic * TOK + h * 512:ic * TOK + (h + 1) * 512],
                            start=(ic == 0), stop=(ic == NCH - 1))
                return last
            em.step("PE", mmo, name=f"p4_mm{oc}",
                    waits=("p4_oo", f"p4_wa{oc}", f"p4_cv{oc}")
                    + ((f"p4_t{oc-2}",) if oc >= 2 else ()))
            em.step("ACT", lambda a, oc=oc: a.activation(
                t05[oc % 2][:, :], op_ps[oc % 2][:, :], AF.Identity,
                bias=bout_sb[:, oc:oc + 1]), name=f"p4_t{oc}",
                waits=(f"p4_mm{oc}",)
                + ((f"p4_acc{oc-2}",) if oc >= 2 else ()))
            em.step("DVE", lambda v, oc=oc: v.tensor_tensor(
                Rh[0][:, oc * TOK:(oc + 1) * TOK], Rh[0][:, oc * TOK:(oc + 1) * TOK],
                t05[oc % 2][:, :], ALU.add), name=f"p4_acc{oc}",
                waits=(f"p4_t{oc}", "p4_rld"))
        em.step("GP", lambda g: g.wait_ge(em.sems["DW"], em.c["DW"]) and None,
                inc=0)
        em.step("SY", lambda s: s.wait_ge(em.sems["DS"], em.c["DS"]) and None,
                inc=0)

    def pdump_spill(em):
        em.step("GP", lambda g: g.dma_start(out=y_d[:, :], in_=r_spill[:, :]))
        em.step("GP", lambda g: g.wait_ge(em.sems["DW"], em.c["DW"]) and None,
                inc=0)

    def pdump(em):
        for c in range(NCH):
            em.step("GP", lambda g, c=c: g.dma_start(
                out=y_d[c * 128:(c + 1) * 128, :],
                in_=Rh[0][:, c * TOK:(c + 1) * TOK]), name=f"dump{c}")
        if _DUMPQKV:
            em.step("SY", lambda s: s.dma_start(out=dbgq_d[:, :],
                                                in_=qk_spill[:, :]))
            em.step("SY", lambda s: s.dma_start(out=dbgv_d[:, :],
                                                in_=v_spill[:, :]))
        if _DUMPO:
            em.step("SY", lambda s: s.dma_start(out=dbgo_d[:, :],
                                                in_=o_bounce[:, :]))
        if _DUMPQKV or _DUMPO:
            em.step("SY", lambda s: s.wait_ge(em.sems["DS"], em.c["DS"])
                    and None, inc=0)
        em.step("GP", lambda g: g.wait_ge(em.sems["DW"], em.c["DW"]) and None,
                inc=0)

    with stack:
        with ExitStack() as stA:
            Rh[0] = stA.enter_context(
                nc.sbuf_tensor("R", [128, NCH * TOK], BF16))
            run_phase("p0", p0)
            if _PHASES >= 1:
                with ExitStack() as st:
                    t1 = make_ffn_tensors(st, "f1", True)
                    run_phase("p1",
                              lambda em: ffn_phase(em, "f1", t1, final=False))
            if _PHASES >= 2:
                with ExitStack() as st:
                    t2 = p2a_make(st)
                    run_phase("p2a", lambda em: p2a(em, t2))
            if _PHASES <= 1:
                run_phase("pdump", pdump)
        Rh[0] = None
        if _PHASES >= 2:
            with ExitStack() as st:
                t2 = p2b_make(st)
                run_phase("p2b", lambda em: p2b(em, t2))
        if _PHASES >= 3:
            with ExitStack() as st:
                t3 = p3_make(st)
                run_phase("p3", lambda em: p3(em, t3))
        with ExitStack() as stB:
            if _PHASES >= 4:
                Rh[0] = stB.enter_context(
                    nc.sbuf_tensor("R2", [128, NCH * TOK], BF16))
                with ExitStack() as st:
                    t4 = p4_make(st)
                    run_phase("p4", lambda em: p4(em, t4))
            if _PHASES >= 5:
                with ExitStack() as st:
                    t5 = make_ffn_tensors(st, "f5", False)
                    run_phase("p5",
                              lambda em: ffn_phase(em, "f5", t5, final=True))
            elif _PHASES >= 4:
                run_phase("pdump", pdump)
            elif _PHASES >= 2:
                run_phase("pdump_s", pdump_spill)
    return nc



def _stage_inputs(x, wqkv, bqkv, wout, bout, up, down, perm):
    sched = attn_schedule()
    bkeys = bias_keys(sched)
    nbk = len(bkeys)
    slopes = _slopes()
    wqkv_p = wqkv[:, perm]
    w2t = np.ascontiguousarray(wout[perm, :].T)
    bout_p = bout[perm]

    def tileK(a, kchunks, width):
        return np.ascontiguousarray(
            a.reshape(kchunks, 128, width).transpose(1, 0, 2)
            .reshape(128, kchunks * width))

    up_img = np.concatenate([tileK(up[h], 2, 1024)[None] for h in range(FH)],
                            axis=0).reshape(FH * 128, 2048)
    dn_img = np.concatenate([tileK(down[h], 8, 256)[None] for h in range(FH)],
                            axis=0).reshape(FH * 128, 2048)
    mask = np.where(np.arange(128)[:, None] <= np.arange(128)[None, :],
                    0.0, NEG).astype(np.float32)
    onesc = np.ones((128, 128), np.float32)
    ones1 = np.ones((1, 128), np.float32)

    ins = []
    for c in range(8):
        b, r = c // 4, c % 4
        heads = [4 * jj + r for jj in range(NSLOT)]
        x_in = np.ascontiguousarray(x[b, TOK * r:TOK * (r + 1), :][:, perm].T)
        qk_blocks = []
        for jj in range(NSLOT):
            h = heads[jj]
            A = wqkv_p[128 * h:128 * h + 128, :]
            qk_blocks.append(tileK(np.ascontiguousarray(A.T), 32, 128))
        for jj in range(NSLOT):
            h = heads[jj]
            A = wqkv_p[4096 + 128 * h:4096 + 128 * h + 128, :]
            qk_blocks.append(tileK(np.ascontiguousarray(A.T), 32, 128))
        w1qk = np.concatenate([bb[None] for bb in qk_blocks], axis=0)\
            .reshape(16 * 128, 4096)
        W1v = np.ascontiguousarray(np.concatenate(
            [wqkv_p[8192 + 128 * h:8192 + 128 * h + 128, :] for h in heads],
            axis=0).T)
        w1v = np.concatenate(
            [tileK(np.ascontiguousarray(W1v[:, 256 * vg:256 * (vg + 1)]),
                   32, 256)[None] for vg in range(4)],
            axis=0).reshape(4 * 128, 8192)
        w2_ord = np.empty_like(w2t)
        for ic in range(NCH):
            h = 4 * (ic // 4) + ic % 4
            w2_ord[128 * ic:128 * (ic + 1), :] = w2t[128 * h:128 * (h + 1), :]
        w2_img = np.concatenate(
            [tileK(np.ascontiguousarray(w2_ord[:, 128 * oc:128 * (oc + 1)]),
                   32, 128)[None] for oc in range(NCH)],
            axis=0).reshape(32 * 128, 4096)
        bqk = np.empty((128, 16), np.float32)
        for jj in range(NSLOT):
            bqk[:, jj] = bqkv[128 * heads[jj]:128 * heads[jj] + 128]
            bqk[:, 8 + jj] = bqkv[4096 + 128 * heads[jj]:
                                  4096 + 128 * heads[jj] + 128]
        bv_ord = np.concatenate(
            [bqkv[8192 + 128 * h:8192 + 128 * h + 128] for h in heads])
        bvb = np.ascontiguousarray(
            np.broadcast_to(bv_ord[None, :], (128, 1024))).astype(np.float32)
        bout_img = np.ascontiguousarray(
            bout_p.reshape(32, 128).T).astype(np.float32)
        btab = np.zeros((128, nbk), np.float32)
        for (jj, dk), idx in bkeys.items():
            btab[:, idx] = (slopes[heads[jj]]
                            * (np.arange(128) + dk)).astype(np.float32)
        etab = np.zeros((128, 8 * 512), np.float32)
        qq = np.arange(512)
        for jj in range(NSLOT):
            W = W_SLOT[jj]
            qref = (qq // W + 1) * W - 1
            ev = np.exp(slopes[heads[jj]] * (qref - qq)).astype(np.float32)
            etab[:, jj * 512:(jj + 1) * 512] = ev[None, :]
        ins.append({
            "x_in": x_in, "w1qk": w1qk, "w1v": w1v, "w2": w2_img,
            "up_w": up_img, "dn_w": dn_img, "bqk": bqk, "bvb": bvb,
            "bout": bout_img, "btab": btab, "etab": etab, "maskt": mask,
            "onesc": onesc, "onesr": ones1,
            "onecol": np.ones((128, 1), np.float32),
        })
    return ins, nbk


def kernel(L0_attn_x, L0_attn_w_qkv, L0_attn_b_qkv, L0_attn_w_out,
           L0_attn_b_out, L0_ffn_up, L0_ffn_down, L0_ffn_perm):
    x = np.asarray(L0_attn_x, np.float32)
    wqkv = np.asarray(L0_attn_w_qkv, np.float32)
    bqkv = np.asarray(L0_attn_b_qkv, np.float32)
    wout = np.asarray(L0_attn_w_out, np.float32)
    bout = np.asarray(L0_attn_b_out, np.float32)
    up = np.asarray(L0_ffn_up, np.float32)
    down = np.asarray(L0_ffn_down, np.float32)
    perm = np.asarray(L0_ffn_perm).astype(np.int64)
    unperm = np.argsort(perm)

    ins, nbk = _stage_inputs(x, wqkv, bqkv, wout, bout, up, down, perm)
    nc = build_program(nbk)
    res = run_bass_kernel_spmd(nc, ins, core_ids=list(range(8)))

    out = np.empty((B, S, D), np.float32)
    for c in range(8):
        b, r = c // 4, c % 4
        y = np.asarray(res.results[c]["y"], np.float32)
        out[b, TOK * r:TOK * (r + 1), :] = y.T[:, unperm]
    return out


# version 15
# speedup vs baseline: 1.0648x; 1.0100x over previous
import math
import numpy as np

import concourse.bass as bass
import concourse.mybir as mybir
from concourse.bass_utils import run_bass_kernel_spmd

F32 = mybir.dt.float32
BF16 = mybir.dt.bfloat16
AF = mybir.ActivationFunctionType
ALU = mybir.AluOpType
ET = mybir.EngineType

B, S, D = 2, 4096, 4096
NHEAD, HD = 32, 128
GROUP = 4
TOK = S // GROUP
NCH = D // 128
NSLOT = 8
SCALE = 1.0 / math.sqrt(HD)
FH, FDH, FM = 16, 256, 1024
EPS = 1e-5
NEG = -1.0e9

_PHASES = 5
_P2QK = 64
_P2V = True
_DUMPQKV = False
_DUMPO = False

W_SLOT = [64, 128, 256, 512, 512, 512, 512, 512]
SLOPE_MIN = [2.0 ** (-(j + 1)) for j in range(8)]
SKIP_MARGIN = 95.0


def _slopes():
    m = (np.arange(1, 33) * (8.0 / 32)).astype(np.float64)
    return 1.0 / (2.0 ** m)


def attn_schedule():
    sched = []
    for j in range(NSLOT):
        W = W_SLOT[j]
        smin = SLOPE_MIN[j]
        per_qc = []
        for qc in range(8):
            items = []
            for kt in range(4 * qc + 4):
                qlo_c = max(0, 128 * kt - 512 * qc)
                sl = []
                a0 = (qlo_c // W) * W
                for a in range(a0, 512, W):
                    b = a + W
                    aa = max(a, qlo_c)
                    if aa >= b:
                        continue
                    dist = (512 * qc + aa) - (128 * kt + 127)
                    if dist * smin > SKIP_MARGIN:
                        continue
                    qref = 512 * qc + b - 1
                    sl.append((aa, b, 128 * kt - qref))
                if sl:
                    items.append((kt, sl[0][0], sl))
            per_qc.append(items)
        sched.append(per_qc)
    return sched


def bias_keys(sched):
    keys = []
    seen = set()
    for j in range(NSLOT):
        for qc in range(8):
            for _, _, sl in sched[j][qc]:
                for _, _, d in sl:
                    if (j, d) not in seen:
                        seen.add((j, d))
                        keys.append((j, d))
    return {k: i for i, k in enumerate(keys)}



ENGS = ("PE", "ACT", "DVE", "GP", "SY")
DEF_SEM = {"PE": "PE", "ACT": "ACT", "DVE": "DVE", "GP": "DW", "SY": "DS"}


class Em:
    def __init__(self, sems):
        self.sems = sems
        self.c = {k: 0 for k in sems}
        self.mil = {}
        self.engine = "SIM"
        self.eng = None
        self.last_wait = {}

    def snap(self):
        return dict(self.c)

    def restore(self, snap):
        self.c = dict(snap)
        self.last_wait = {}

    def wait(self, *names):
        for n in names:
            semname, val = self.mil[n]
            if self.eng is None:
                continue
            if DEF_SEM.get(self.engine) == semname:
                continue
            if self.last_wait.get(semname, -1) >= val:
                continue
            self.last_wait[semname] = val
            self.eng.wait_ge(self.sems[semname], val)

    def wait_own(self, name):
        semname, val = self.mil[name]
        if self.eng is not None:
            if self.last_wait.get(semname, -1) < val:
                self.last_wait[semname] = val
                self.eng.wait_ge(self.sems[semname], val)

    def step(self, engine, fn, name=None, waits=(), inc=None, sem=None,
             own_waits=()):
        sem = sem or DEF_SEM[engine]
        inc = inc if inc is not None else (16 if sem in ("DW", "DS", "DA")
                                           else 1)
        self.c[sem] += inc
        if name is not None:
            prev = self.mil.get(name)
            assert prev is None or prev == (sem, self.c[sem]), (name, prev)
            self.mil[name] = (sem, self.c[sem])
        if self.engine != engine:
            return
        self.wait(*waits)
        for w in own_waits:
            self.wait_own(w)
        last = fn(self.eng)
        if last is not None:
            last.then_inc(self.sems[sem], inc)

    def raw(self, engine, fn, name=None, waits=(), inc=0, sem=None):
        sem = sem or DEF_SEM[engine]
        self.c[sem] += inc
        if name is not None:
            self.mil[name] = (sem, self.c[sem])
        if self.engine != engine:
            return
        self.wait(*waits)
        fn(self.eng, self.sems[sem])



def build_program(nbk):
    sched = attn_schedule()
    bkeys = bias_keys(sched)
    assert nbk == len(bkeys)

    nc = bass.Bass("TRN2", target_bir_lowering=False)

    x_d = nc.dram_tensor("x_in", [D, TOK], F32, kind="ExternalInput")
    w1qk_d = nc.dram_tensor("w1qk", [16 * 128, 4096], F32, kind="ExternalInput")
    w1v_d = nc.dram_tensor("w1v", [4 * 128, 8192], F32, kind="ExternalInput")
    w2_d = nc.dram_tensor("w2", [32 * 128, 4096], F32, kind="ExternalInput")
    up_d = nc.dram_tensor("up_w", [16 * 128, 2048], F32, kind="ExternalInput")
    dn_d = nc.dram_tensor("dn_w", [16 * 128, 2048], F32, kind="ExternalInput")
    bqk_d = nc.dram_tensor("bqk", [128, 16], F32, kind="ExternalInput")
    bv_d = nc.dram_tensor("bvb", [128, 1024], F32, kind="ExternalInput")
    bout_d = nc.dram_tensor("bout", [128, 32], F32, kind="ExternalInput")
    btab_d = nc.dram_tensor("btab", [128, nbk], F32, kind="ExternalInput")
    etab_d = nc.dram_tensor("etab", [128, 8 * 512], F32, kind="ExternalInput")
    mask_d = nc.dram_tensor("maskt", [128, 128], F32, kind="ExternalInput")
    onesc_d = nc.dram_tensor("onesc", [128, 128], F32, kind="ExternalInput")
    onecol_d = nc.dram_tensor("onecol", [128, 1], F32, kind="ExternalInput")
    ones1_d = nc.dram_tensor("onesr", [1, 128], F32, kind="ExternalInput")
    y_d = nc.dram_tensor("y", [D, TOK], F32, kind="ExternalOutput")
    if _DUMPQKV:
        dbgq_d = nc.dram_tensor("dbgq", [16 * 128, S], BF16,
                                kind="ExternalOutput")
        dbgv_d = nc.dram_tensor("dbgv", [S, 1024], BF16,
                                kind="ExternalOutput")
    if _DUMPO:
        dbgo_d = nc.dram_tensor("dbgo", [NSLOT * 128, S], BF16,
                                kind="ExternalOutput")

    xln_bounce = nc.dram_tensor("xln_bounce", [D, TOK], BF16)
    xln_gather = nc.dram_tensor("xln_gather", [GROUP * D, TOK], BF16)
    qk_spill = nc.dram_tensor("qk_spill", [16 * 128, S], BF16)
    v_spill = nc.dram_tensor("v_spill", [S, 1024], BF16)
    o_bounce = nc.dram_tensor("o_bounce", [NSLOT * 128, S], BF16)
    o_gather = nc.dram_tensor("o_gather", [GROUP * NSLOT * 128, S], BF16)
    r_spill = nc.dram_tensor("r_spill", [D, TOK], BF16)
    upbf_d = nc.dram_tensor("upbf", [16 * 128, 2048], BF16)
    dnbf_d = nc.dram_tensor("dnbf", [16 * 128, 2048], BF16)
    wqbf_d = nc.dram_tensor("wqbf", [16 * 128, 4096], BF16)
    wvbf_d = nc.dram_tensor("wvbf", [4 * 128, 8192], BF16)

    sem_names = ("PE", "ACT", "DVE", "DW", "DS", "DA", "CC1", "CC2")

    from contextlib import ExitStack
    stack = ExitStack()
    sems = {n: stack.enter_context(nc.semaphore("s" + n)) for n in sem_names}
    em = Em(sems)

    Rh = [None]
    mask_sb = stack.enter_context(nc.sbuf_tensor("mask_sb", [128, 128], F32))
    ones_bf = stack.enter_context(nc.sbuf_tensor("ones_bf", [128, 128], BF16))
    onec_bf = stack.enter_context(nc.sbuf_tensor("onec_bf", [128, 1], BF16))
    ones1_sb = stack.enter_context(nc.sbuf_tensor("ones1_sb", [1, 128], F32))
    bqk_sb = stack.enter_context(nc.sbuf_tensor("bqk_sb", [128, 16], F32))
    bout_sb = stack.enter_context(nc.sbuf_tensor("bout_sb", [128, 32], F32))
    btab_sb = stack.enter_context(nc.sbuf_tensor("btab_sb", [128, nbk], F32))

    def run_phase(name, fn):
        snap = em.snap()
        em.engine, em.eng = "SIM", None
        fn(em)
        with nc.Block(name) as blk:
            for engname, reg in (("PE", blk.tensor), ("ACT", blk.scalar),
                                 ("DVE", blk.vector), ("GP", blk.gpsimd),
                                 ("SY", blk.sync)):
                def body(eng, engname=engname):
                    em.restore(snap)
                    em.engine, em.eng = engname, eng
                    fn(em)
                reg(body)
        em.restore(snap)
        em.engine, em.eng = "SIM", None
        fn(em)

    def p0(em):
        xap = x_d[:, :].rearrange("(c p) t -> p c t", p=128)
        rap = Rh[0][:, :].rearrange("p (c t) -> p c t", c=NCH)
        for q in range(8):
            em.step("GP", lambda g, q=q: g.dma_start(
                out=rap[:, q * 4:(q + 1) * 4, :],
                in_=xap[:, q * 4:(q + 1) * 4, :]),
                name=f"ld_x{q}")
        em.step("SY", lambda s: s.dma_start(out=mask_sb[:, :], in_=mask_d[:, :]))
        em.step("GP", lambda g: g.dma_start(out=ones_bf[:, :], in_=onesc_d[:, :]))
        em.step("GP", lambda g: g.dma_start(out=onec_bf[:, :], in_=onecol_d[:, :]))
        em.step("SY", lambda s: s.dma_start(out=ones1_sb[:, :], in_=ones1_d[:, :]))
        em.step("SY", lambda s: s.dma_start(out=bqk_sb[:, :], in_=bqk_d[:, :]))
        em.step("SY", lambda s: s.dma_start(out=bout_sb[:, :], in_=bout_d[:, :]))
        em.step("SY", lambda s: s.dma_start(out=btab_sb[:, :], in_=btab_d[:, :]),
                name="ld_consts")
        em.step("GP", lambda g: g.wait_ge(em.sems["DW"], em.c["DW"]) and None,
                inc=0)
        em.step("SY", lambda s: s.wait_ge(em.sems["DS"], em.c["DS"]) and None,
                inc=0)
        em.step("ACT", lambda a: a.wait_ge(em.sems["DA"], em.c["DA"]) and None,
                inc=0)

    def ln_emit(em, tag, t, apply_only_stats=True):
        (sq_sb, row_mean, row_ms, row_rstd, row_mrs,
         sum_ps, sq_ps, bc_ps, rstdb_sb, mrsb_sb) = t
        for c in range(NCH):
            em.step("DVE", lambda v, c=c: v.tensor_tensor(
                sq_sb[:, (c % 2) * TOK:(c % 2 + 1) * TOK],
                Rh[0][:, c * TOK:(c + 1) * TOK],
                Rh[0][:, c * TOK:(c + 1) * TOK], ALU.mult),
                name=f"{tag}_sq{c}",
                waits=(f"{tag}_st{c-2}",) if c >= 2 else ())
            def mm(tt, c=c):
                for h in range(2):
                    tt.matmul(sum_ps[0:1, h * 512:(h + 1) * 512],
                              onec_bf[:, :],
                              Rh[0][:, c * TOK + h * 512: c * TOK + (h + 1) * 512],
                              start=(c == 0), stop=(c == NCH - 1))
                last = None
                for h in range(2):
                    last = tt.matmul(
                        sq_ps[0:1, h * 512:(h + 1) * 512], onec_bf[:, :],
                        sq_sb[:, (c % 2) * TOK + h * 512:
                              (c % 2) * TOK + (h + 1) * 512],
                        start=(c == 0), stop=(c == NCH - 1))
                return last
            em.step("PE", mm, name=f"{tag}_st{c}", waits=(f"{tag}_sq{c}",))
        em.step("DVE", lambda v: v.tensor_scalar_mul(
            row_mean[:, :], sum_ps[0:1, :], 1.0 / D),
            name=f"{tag}_mean", waits=(f"{tag}_st{NCH-1}",))
        em.step("DVE", lambda v: v.tensor_scalar_mul(
            row_ms[:, :], sq_ps[0:1, :], 1.0 / D), name=f"{tag}_ms")
        em.step("ACT", lambda a: a.activation(
            row_rstd[:, :], row_mean[:, :], AF.Square),
            name=f"{tag}_m2", waits=(f"{tag}_mean",))
        em.step("DVE", lambda v: v.tensor_tensor(
            row_ms[:, :], row_ms[:, :], row_rstd[:, :], ALU.subtract),
            name=f"{tag}_var0", waits=(f"{tag}_m2",), own_waits=(f"{tag}_ms",))
        em.step("DVE", lambda v: v.tensor_scalar_add(
            row_ms[:, :], row_ms[:, :], EPS),
            name=f"{tag}_var", own_waits=(f"{tag}_var0",))
        em.step("ACT", lambda a: a.activation(
            row_rstd[:, :], row_ms[:, :], AF.Ln),
            name=f"{tag}_ln", waits=(f"{tag}_var",))
        em.step("ACT", lambda a: a.activation(
            row_rstd[:, :], row_rstd[:, :], AF.Exp, scale=-0.5),
            name=f"{tag}_rstd", own_waits=(f"{tag}_ln",))
        em.step("DVE", lambda v: v.tensor_tensor(
            row_mrs[:, :], row_mean[:, :], row_rstd[:, :], ALU.mult),
            name=f"{tag}_mrs", waits=(f"{tag}_rstd",),
            own_waits=(f"{tag}_mean",))
        def bc(tt):
            last = None
            for h in range(2):
                tt.matmul(bc_ps[:, h * 512:(h + 1) * 512], ones1_sb[:, :],
                          row_rstd[0:1, h * 512:(h + 1) * 512],
                          start=True, stop=True)
                last = tt.matmul(bc_ps[:, 1024 + h * 512:1024 + (h + 1) * 512],
                                 ones1_sb[:, :],
                                 row_mrs[0:1, h * 512:(h + 1) * 512],
                                 start=True, stop=True)
            return last
        em.step("PE", bc, name=f"{tag}_bc", waits=(f"{tag}_mrs",))
        em.step("DVE", lambda v: v.tensor_copy(rstdb_sb[:, :], bc_ps[:, 0:1024]),
                name=f"{tag}_rb", waits=(f"{tag}_bc",))
        em.step("DVE", lambda v: v.tensor_copy(mrsb_sb[:, :], bc_ps[:, 1024:2048]),
                name=f"{tag}_mb")

    def ln_apply(em, tag, c, rstdb_sb, mrsb_sb, tscratch, dst, dsl,
                 extra_waits=()):
        em.step("DVE", lambda v: v.tensor_tensor(
            tscratch[:, :], Rh[0][:, c * TOK:(c + 1) * TOK], rstdb_sb[:, :],
            ALU.mult), name=f"{tag}_xa{c}",
            waits=(f"{tag.split('_')[0]}_rb", f"{tag.split('_')[0]}_mb")
            + tuple(extra_waits))
        em.step("DVE", lambda v: v.tensor_tensor(
            dst[:, dsl], tscratch[:, :], mrsb_sb[:, :], ALU.subtract),
            name=f"{tag}_xln{c}", own_waits=(f"{tag}_xa{c}",))

    NWB = 5

    def ffn_phase(em, tag, t, final):
        (lnt, up_sb, dn_sb, dnf_sb, hid_sb, xln_scr, lnsc, t05, out_stg,
         up_ps, dn_ps) = t
        ln_emit(em, tag, lnt)

        def dn_block(hf):
            for oc2 in range(2):
                def mmdn(tt, hf=hf, oc2=oc2):
                    last = None
                    for ic2 in range(8):
                        for h in range(2):
                            last = tt.matmul(
                                dn_ps[oc2 % 2][:, h * 512:(h + 1) * 512],
                                dn_sb[hf % NWB][:, ic2 * 256 + oc2 * 128:
                                              ic2 * 256 + oc2 * 128 + 128],
                                hid_sb[hf % 2][:, ic2 * TOK + h * 512:
                                       ic2 * TOK + (h + 1) * 512],
                                start=(ic2 == 0), stop=(ic2 == 7))
                    return last
                em.step("PE", mmdn, name=f"{tag}_dn{hf}_{oc2}",
                        waits=(f"{tag}_wdn{hf}", f"{tag}_gelu{hf}_7")
                        + ((f"{tag}_acc{hf}_{oc2-1}",) if oc2 >= 1 else
                           ((f"{tag}_acc{hf-1}_1",) if hf >= 1 else ())))
                em.step("DVE", lambda v, hf=hf, oc2=oc2: v.tensor_scalar_mul(
                    t05[oc2 % 2][:, :], dn_ps[oc2 % 2][:, :], 0.5),
                    name=f"{tag}_t05_{hf}_{oc2}",
                    waits=(f"{tag}_dn{hf}_{oc2}",))
                c = 2 * hf + oc2
                if not final:
                    em.step("DVE", lambda v, c=c, oc2=oc2: v.tensor_tensor(
                        Rh[0][:, c * TOK:(c + 1) * TOK],
                        Rh[0][:, c * TOK:(c + 1) * TOK],
                        t05[oc2 % 2][:, :], ALU.add),
                        name=f"{tag}_acc{hf}_{oc2}",
                        waits=(f"{tag}_t05_{hf}_{oc2}",))
                else:
                    em.step("DVE", lambda v, c=c, oc2=oc2: v.tensor_tensor(
                        out_stg[oc2 % 2][:, :], Rh[0][:, c * TOK:(c + 1) * TOK],
                        t05[oc2 % 2][:, :], ALU.add),
                        name=f"{tag}_acc{hf}_{oc2}",
                        waits=(f"{tag}_t05_{hf}_{oc2}",)
                        + ((f"{tag}_yd{hf}_{oc2-1}",) if oc2 >= 1 else
                           ((f"{tag}_yd{hf-1}_1",) if hf >= 1 else ())))
                    em.step("SY", lambda s, c=c, oc2=oc2: s.dma_start(
                        out=y_d[c * 128:(c + 1) * 128, :],
                        in_=out_stg[oc2 % 2][:, :]),
                        name=f"{tag}_yd{hf}_{oc2}",
                        waits=(f"{tag}_acc{hf}_{oc2}",))
            em.mil[f"{tag}_dn_done{hf}"] = em.mil[f"{tag}_dn{hf}_1"]

        for hf in range(FH):
            if not final:
                em.step("GP", lambda g, hf=hf: g.dma_start(
                    out=up_sb[hf % NWB][:, :],
                    in_=up_d[hf * 128:(hf + 1) * 128, :]),
                    name=f"{tag}_wup{hf}",
                    waits=((f"{tag}_up_done{hf-NWB}", f"{tag}_spu{hf-NWB}")
                           if hf >= NWB else ()))
                em.step("SY", lambda s, hf=hf: s.dma_start(
                    out=upbf_d[hf * 128:(hf + 1) * 128, :],
                    in_=up_sb[hf % NWB][:, :]), name=f"{tag}_spu{hf}",
                    waits=(f"{tag}_wup{hf}",))
                em.step("SY", lambda s, hf=hf: s.dma_start(
                    out=dnf_sb[hf % 2][:, :],
                    in_=dn_d[hf * 128:(hf + 1) * 128, :]),
                    name=f"{tag}_wdf{hf}",
                    waits=(f"{tag}_wdn{hf-2}",) if hf >= 2 else ())
                em.step("DVE", lambda v, hf=hf: v.tensor_copy(
                    dn_sb[hf % NWB][:, :], dnf_sb[hf % 2][:, :]),
                    name=f"{tag}_wdn{hf}",
                    waits=(f"{tag}_wdf{hf}",)
                    + ((f"{tag}_dn_done{hf-NWB}", f"{tag}_spd{hf-NWB}")
                       if hf >= NWB else ()))
                em.step("SY", lambda s, hf=hf: s.dma_start(
                    out=dnbf_d[hf * 128:(hf + 1) * 128, :],
                    in_=dn_sb[hf % NWB][:, :]), name=f"{tag}_spd{hf}",
                    waits=(f"{tag}_wdn{hf}",))
            else:
                em.step("GP", lambda g, hf=hf: g.dma_start(
                    out=up_sb[hf % NWB][:, :],
                    in_=upbf_d[hf * 128:(hf + 1) * 128, :]),
                    name=f"{tag}_wup{hf}",
                    waits=(f"{tag}_up_done{hf-NWB}",) if hf >= NWB else ())
                em.step("SY", lambda s, hf=hf: s.dma_start(
                    out=dn_sb[hf % NWB][:, :],
                    in_=dnbf_d[hf * 128:(hf + 1) * 128, :]),
                    name=f"{tag}_wdn{hf}",
                    waits=(f"{tag}_dn_done{hf-NWB}",) if hf >= NWB else ())
            hp = (hf % 2) * 2 * TOK
            if hf == 0:
                for ci in range(2):
                    ln_apply(em, f"{tag}_h0", ci, lnt[-2], lnt[-1],
                             lnsc, xln_scr, slice(ci * TOK, (ci + 1) * TOK))
            for oc in range(8):
                def mmup(tt, hf=hf, oc=oc, hp=hp):
                    last = None
                    for ci in range(2):
                        for h in range(2):
                            last = tt.matmul(
                                up_ps[oc % 2][:, h * 512:(h + 1) * 512],
                                up_sb[hf % NWB][:, ci * 1024 + oc * 128:
                                              ci * 1024 + oc * 128 + 128],
                                xln_scr[:, hp + ci * TOK + h * 512:
                                        hp + ci * TOK + (h + 1) * 512],
                                start=(ci == 0), stop=(ci == 1))
                    return last
                em.step("PE", mmup, name=f"{tag}_up{hf}_{oc}",
                        waits=(f"{tag}_wup{hf}", f"{tag}_h{hf}_xln{2*hf+1}")
                        + ((f"{tag}_gelu{hf}_{oc-2}",) if oc >= 2 else
                           ((f"{tag}_gelu{hf-1}_{oc+6}",) if hf >= 1 else ())))
                em.step("ACT", lambda a, hf=hf, oc=oc: a.activation(
                    hid_sb[hf % 2][:, oc * TOK:(oc + 1) * TOK],
                    up_ps[oc % 2][:, :], AF.Gelu),
                    name=f"{tag}_gelu{hf}_{oc}", waits=(f"{tag}_up{hf}_{oc}",)
                    + ((f"{tag}_dn{hf-2}_1",) if hf >= 2 else ()))
            em.mil[f"{tag}_up_done{hf}"] = em.mil[f"{tag}_up{hf}_7"]
            if hf + 1 < FH:
                hp1 = ((hf + 1) % 2) * 2 * TOK
                for ci in range(2):
                    c = 2 * (hf + 1) + ci
                    ln_apply(em, f"{tag}_h{hf+1}", c, lnt[-2], lnt[-1],
                             lnsc, xln_scr, slice(hp1 + ci * TOK,
                                                  hp1 + (ci + 1) * TOK),
                             extra_waits=((f"{tag}_up{hf-1}_7",)
                                          if hf >= 1 and ci == 0 else ()))
            if hf >= 1:
                dn_block(hf - 1)
        dn_block(FH - 1)
        em.step("SY", lambda s: s.wait_ge(em.sems["DS"], em.c["DS"]) and
                None, inc=0)
        em.step("GP", lambda g: g.wait_ge(em.sems["DW"], em.c["DW"]) and None,
                inc=0)
        em.step("ACT", lambda a: a.wait_ge(em.sems["DA"], em.c["DA"]) and None,
                inc=0)

    def make_ln_tensors(st, pfx, big, big2):
        sq_sb = st.enter_context(nc.sbuf_tensor(pfx + "sq", [128, 2 * TOK], BF16))
        rows = [st.enter_context(nc.sbuf_tensor(pfx + n, [1, TOK], F32))
                for n in ("mean", "ms", "rstd", "mrs")]
        sum_ps = big2[0:1, 0:TOK]
        sq_ps = big2[0:1, TOK:2 * TOK]
        bc_ps = big[:, 0:2 * TOK]
        rstdb = st.enter_context(nc.sbuf_tensor(pfx + "rstdb", [128, TOK], F32))
        mrsb = st.enter_context(nc.sbuf_tensor(pfx + "mrsb", [128, TOK], F32))
        return (sq_sb, *rows, sum_ps, sq_ps, bc_ps, rstdb, mrsb)

    def make_ffn_tensors(st, pfx, stage_dn):
        big = st.enter_context(nc.psum_tensor(pfx + "bigA", [128, 2 * TOK], F32))
        big2 = st.enter_context(nc.psum_tensor(pfx + "bigB", [128, 2 * TOK], F32))
        lnt = make_ln_tensors(st, pfx + "ln", big, big2)
        up_sb = [st.enter_context(nc.sbuf_tensor(f"{pfx}up{i}", [128, 2048], BF16))
                 for i in range(NWB)]
        dn_sb = [st.enter_context(nc.sbuf_tensor(f"{pfx}dn{i}", [128, 2048], BF16))
                 for i in range(NWB)]
        dnf_sb = ([st.enter_context(
            nc.sbuf_tensor(f"{pfx}dnf{i}", [128, 2048], F32))
            for i in range(2)] if stage_dn else None)
        hid_sb = [st.enter_context(
            nc.sbuf_tensor(f"{pfx}hid{i}", [128, 8 * TOK], BF16))
            for i in range(2)]
        xln_scr = st.enter_context(nc.sbuf_tensor(pfx + "xln", [128, 4 * TOK], BF16))
        lnsc = st.enter_context(nc.sbuf_tensor(pfx + "lnsc", [128, TOK], F32))
        t05 = [st.enter_context(nc.sbuf_tensor(f"{pfx}t05{i}", [128, TOK], F32))
               for i in range(2)]
        out_stg = ([st.enter_context(
            nc.sbuf_tensor(f"{pfx}os{i}", [128, TOK], F32))
            for i in range(2)] if not stage_dn else None)
        up_ps = [big[:, i * TOK:(i + 1) * TOK] for i in range(2)]
        dn_ps = [big2[:, i * TOK:(i + 1) * TOK] for i in range(2)]
        return (lnt, up_sb, dn_sb, dnf_sb, hid_sb, xln_scr, lnsc, t05, out_stg,
                up_ps, dn_ps)

    def p2a_make(st):
        big = st.enter_context(nc.psum_tensor("p2bigA", [128, 2 * TOK], F32))
        big2 = st.enter_context(nc.psum_tensor("p2bigB", [128, 2 * TOK], F32))
        lnt = make_ln_tensors(st, "l2", big, big2)
        xchunk = [st.enter_context(nc.sbuf_tensor(f"xc{i}", [128, TOK], BF16))
                  for i in range(2)]
        tsc = st.enter_context(nc.sbuf_tensor("l2t", [128, TOK], F32))
        return (lnt, xchunk, tsc)

    def p2b_make(st):
        big = st.enter_context(nc.psum_tensor("p2bigC", [128, 2 * TOK], F32))
        big2 = st.enter_context(nc.psum_tensor("p2bigD", [128, 2 * TOK], F32))
        xg = [st.enter_context(nc.sbuf_tensor(f"xg{i}", [128, NCH * TOK], BF16))
              for i in range(2)]
        wq_sb = [st.enter_context(nc.sbuf_tensor(f"wq{i}", [128, 4096], BF16))
                 for i in range(3)]
        wv_sb = [st.enter_context(nc.sbuf_tensor(f"wv{i}", [128, 8192], BF16))
                 for i in range(2)]
        bv_sb = st.enter_context(nc.sbuf_tensor("bv", [128, 1024], F32))
        qk_stg = [st.enter_context(nc.sbuf_tensor(f"qs{i}", [128, TOK], BF16))
                  for i in range(2)]
        v_stg = [st.enter_context(nc.sbuf_tensor(f"vs{i}", [128, 256], BF16))
                 for i in range(2)]
        qk_ps = [big[:, i * TOK:(i + 1) * TOK] for i in range(2)]
        v_ps = [big2[:, i * 512:i * 512 + 256] for i in range(2)]
        return (xg, wq_sb, wv_sb, bv_sb, qk_stg, v_stg, qk_ps, v_ps)

    def p2a(em, t):
        (lnt, xchunk, tsc) = t
        ln_emit(em, "l2", lnt)
        for c in range(NCH):
            ln_apply(em, f"l2_c", c, lnt[-2], lnt[-1], tsc,
                     xchunk[c % 2], slice(0, TOK),
                     extra_waits=(f"l2_sp{c-2}",) if c >= 2 else ())
            em.step("ACT", lambda a, c=c: a.dma_start(
                out=xln_bounce[c * 128:(c + 1) * 128, :],
                in_=xchunk[c % 2][:, :]), name=f"l2_sp{c}", sem="DA",
                waits=(f"l2_c_xln{c}",))
        em.step("GP", lambda g: g.dma_start(
            out=r_spill[:, :].rearrange("(c p) t -> p c t", p=128),
            in_=Rh[0][:, :].rearrange("p (c t) -> p c t", c=NCH)),
            name="r_spill")
        def cc(g, sem):
            g.wait_ge(em.sems[em.mil[f"l2_sp{NCH-1}"][0]],
                      em.mil[f"l2_sp{NCH-1}"][1])
            g.collective_compute(
                "AllGather", ALU.bypass,
                replica_groups=[[0, 1, 2, 3], [4, 5, 6, 7]],
                ins=[xln_bounce[:, :].opt()],
                outs=[xln_gather[:, :].opt()]).then_inc(sem, 1)
        em.raw("GP", cc, name="cc1", inc=1, sem="CC1")

    def p2b(em, t):
        (xg, wq_sb, wv_sb, bv_sb, qk_stg, v_stg, qk_ps, v_ps) = t
        em.step("SY", lambda s: s.dma_start(out=bv_sb[:, :], in_=bv_d[:, :]),
                name="ld_bv")
        gap = xln_gather[:, :].rearrange("(tt c p) t -> tt p c t",
                                         tt=GROUP, p=128)
        bap = xln_bounce[:, :].rearrange("(c p) t -> p c t", p=128)
        xgv = [x[:, :].rearrange("p (c t) -> p c t", c=NCH) for x in xg]
        rid = {}

        def quarters(rr, u):
            return rr if u == 0 else [q for q in range(4) if q != rr][u - 1]

        def sw_dma(eng, name, wait_names, body_fn):
            def raw(g, sem):
                for wn in wait_names:
                    sname, val = em.mil[wn]
                    g.wait_ge(em.sems[sname], val)
                pid = nc.partition_id([ET.Pool if eng == "GP" else ET.SP])
                reg = g.compute_val(pid)
                for arm in g.Switch(reg, 8):
                    body_fn(g, sem, arm % 4)
            em.raw(eng, raw, name=name, inc=16,
                   sem="DW" if eng == "GP" else "DS")

        _EARLY0 = False
        for tt in range(GROUP):
            if tt == 0 and _EARLY0:
                em.step("SY", lambda s: s.dma_start(
                    out=xgv[0], in_=bap[:, :, :]), name="xg0",
                    waits=(f"l2_sp{NCH-1}",))
            elif tt == 0:
                def xgb0(g, sem, rr):
                    g.dma_start(out=xgv[0],
                                in_=gap[rr, :, :, :]).then_inc(sem, 16)
                sw_dma("GP", "xg0", ("cc1",), xgb0)
            else:
                def xgb(g, sem, rr, tt=tt):
                    g.dma_start(out=xgv[tt % 2],
                                in_=gap[quarters(rr, tt), :, :, :]
                                ).then_inc(sem, 16)
                sw_dma("GP", f"xg{tt}", ("cc1",)
                       + ((f"p2_vdone{tt-2}",) if tt >= 2 else ()), xgb)
            for oc in range(16):
                i = tt * 16 + oc
                if i >= _P2QK:
                    continue
                if tt == 0:
                    em.step("GP", lambda g, oc=oc, i=i: g.dma_start(
                        out=wq_sb[i % 3][:, :],
                        in_=w1qk_d[oc * 128:(oc + 1) * 128, :]),
                        name=f"wq{i}",
                        waits=((f"qkmm{i-3}", f"wqsp{i-3}")
                               if i >= 3 else ()))
                    em.step("ACT", lambda a, oc=oc, i=i: a.dma_start(
                        out=wqbf_d[oc * 128:(oc + 1) * 128, :],
                        in_=wq_sb[i % 3][:, :]), name=f"wqsp{oc}", sem="DA",
                        waits=(f"wq{i}",))
                def mmqk(tt_, i=i, oc=oc):
                    last = None
                    for ic in range(NCH):
                        for h in range(2):
                            last = tt_.matmul(
                                qk_ps[i % 2][:, h * 512:(h + 1) * 512],
                                wq_sb[i % 3][:, ic * 128:(ic + 1) * 128],
                                xg[tt % 2][:, ic * TOK + h * 512:
                                           ic * TOK + (h + 1) * 512],
                                start=(ic == 0), stop=(ic == NCH - 1))
                    return last
                em.step("PE", mmqk, name=f"qkmm{i}",
                        waits=(f"xg{tt}", f"wq{i}")
                        + ((f"qkev{i-2}",) if i >= 2 else ()))
                em.step("DVE", lambda v, i=i, oc=oc: v.tensor_scalar_add(
                    qk_stg[i % 2][:, :], qk_ps[i % 2][:, :],
                    bqk_sb[:, oc:oc + 1]), name=f"qkev{i}",
                    waits=(f"qkmm{i}",)
                    + ((f"qksp{i-2}",) if i >= 2 else ()))
                if 16 <= i + 3 < 64:
                    rw = ((f"wqsp{(i+3) % 16}", f"wqsp{i}") if i < 16
                          else ("wqsp15",))
                    em.step("ACT", lambda a, i=i: a.dma_start(
                        out=wq_sb[(i + 3) % 3][:, :],
                        in_=wqbf_d[((i + 3) % 16) * 128:
                                   ((i + 3) % 16 + 1) * 128, :]),
                        name=f"wq{i+3}", sem="DA",
                        waits=(f"qkmm{i}",) + rw)
            for vg in range(4 if _P2V else 0):
                gi = tt * 4 + vg
                if tt == 0:
                    em.step("GP", lambda g, vg=vg, gi=gi: g.dma_start(
                        out=wv_sb[gi % 2][:, :],
                        in_=w1v_d[vg * 128:(vg + 1) * 128, :]),
                        name=f"wv{gi}",
                        waits=((f"vmm{gi-2}_7", f"wvsp{gi-2}")
                               if gi >= 2 else ()))
                    em.step("ACT", lambda a, vg=vg, gi=gi: a.dma_start(
                        out=wvbf_d[vg * 128:(vg + 1) * 128, :],
                        in_=wv_sb[gi % 2][:, :]), name=f"wvsp{vg}", sem="DA",
                        waits=(f"wv{gi}",))
                for ttile in range(8):
                    vi = gi * 8 + ttile
                    def mmv(tt_, tt=tt, ttile=ttile, vi=vi, gi=gi):
                        last = None
                        for ic in range(NCH):
                            last = tt_.matmul(
                                v_ps[ttile % 2][:, :],
                                xg[tt % 2][:, ic * TOK + ttile * 128:
                                           ic * TOK + ttile * 128 + 128],
                                wv_sb[gi % 2][:, ic * 256:(ic + 1) * 256],
                                start=(ic == 0), stop=(ic == NCH - 1))
                        return last
                    em.step("PE", mmv, name=f"vmm{gi}_{ttile}",
                            waits=(f"wv{gi}", f"xg{tt}")
                            + ((f"vev{vi-2}",) if vi >= 2 else ()))
                    em.step("DVE", lambda v, vi=vi, vg=vg, ttile=ttile:
                            v.tensor_tensor(
                                v_stg[ttile % 2][:, :], v_ps[ttile % 2][:, :],
                                bv_sb[:, vg * 256:(vg + 1) * 256], ALU.add),
                            name=f"vev{vi}",
                            waits=(f"vmm{gi}_{ttile}", "ld_bv")
                            + ((f"vsp{vi-2}",) if vi >= 2 else ()))
                if ttile == 7 and 4 <= gi + 2 < 16:
                    rw = ((f"wvsp{(gi+2) % 4}", f"wvsp{gi}") if gi < 4
                          else ("wvsp3",))
                    em.step("ACT", lambda a, gi=gi: a.dma_start(
                        out=wv_sb[gi % 2][:, :],
                        in_=wvbf_d[((gi + 2) % 4) * 128:
                                   ((gi + 2) % 4 + 1) * 128, :]),
                        name=f"wv{gi+2}", sem="DA",
                        waits=(f"vmm{gi}_7",) + rw)
            if _P2V:
                em.mil[f"p2_vdone{tt}"] = em.mil[f"vmm{tt*4+3}_7"]
            base = em.c["DS"]
            gates = ([em.mil[f"qkev{tt*16+oc}"] for oc in range(16)]
                     + [em.mil[f"vev{(tt*4+vg)*8+tl}"]
                        for vg in range(4) for tl in range(8)])

            def spill_batch(g, sem, tt=tt, gates=gates):
                pid = nc.partition_id([ET.SP])
                reg = g.compute_val(pid)
                for arm in g.Switch(reg, 8):
                    tq = quarters(arm % 4, tt)
                    k = 0
                    for oc in range(16):
                        i = tt * 16 + oc
                        g.wait_ge(em.sems[gates[k][0]], gates[k][1])
                        k += 1
                        g.dma_start(
                            out=qk_spill[oc * 128:(oc + 1) * 128,
                                         tq * TOK:(tq + 1) * TOK],
                            in_=qk_stg[i % 2][:, :]).then_inc(sem, 16)
                    for vg in range(4):
                        for tl in range(8):
                            g.wait_ge(em.sems[gates[k][0]], gates[k][1])
                            k += 1
                            g.dma_start(
                                out=v_spill[tq * TOK + tl * 128:
                                            tq * TOK + (tl + 1) * 128,
                                            vg * 256:(vg + 1) * 256],
                                in_=v_stg[tl % 2][:, :]).then_inc(sem, 16)
            em.raw("SY", spill_batch, name=f"spb{tt}", inc=16 * 48, sem="DS")
            for oc in range(16):
                em.mil[f"qksp{tt*16+oc}"] = ("DS", base + 16 * (oc + 1))
            for vg in range(4):
                for tl in range(8):
                    em.mil[f"vsp{(tt*4+vg)*8+tl}"] = (
                        "DS", base + 256 + 16 * (vg * 8 + tl + 1))
        em.step("GP", lambda g: g.wait_ge(em.sems["DW"], em.c["DW"]) and None,
                inc=0)
        em.step("SY", lambda s: s.wait_ge(em.sems["DS"], em.c["DS"]) and None,
                inc=0)
        em.step("ACT", lambda a: a.wait_ge(em.sems["DA"], em.c["DA"]) and None,
                inc=0)

    NB = 4
    DEPTH = 3

    def p3_make(st):
        q_sb = [st.enter_context(nc.sbuf_tensor(f"q{i}", [128, S], BF16))
                for i in range(2)]
        k_sb = [st.enter_context(nc.sbuf_tensor(f"k{i}", [128, S], BF16))
                for i in range(2)]
        v_sb = st.enter_context(nc.sbuf_tensor("vall", [128, 32 * 1024], BF16))
        e_sb = st.enter_context(nc.sbuf_tensor("e_sb", [128, 8 * 512], F32))
        pbar = [st.enter_context(nc.sbuf_tensor(f"pb{i}", [128, 512], BF16))
                for i in range(NB)]
        nsc = [st.enter_context(nc.sbuf_tensor(f"nsc{i}", [128, 512], F32))
               for i in range(4)]
        o_stg = [st.enter_context(nc.sbuf_tensor(f"og{i}", [128, 512], BF16))
                 for i in range(2)]
        s_ps = [st.enter_context(nc.psum_tensor(f"sps{i}", [128, 512], F32))
                for i in range(NB)]
        o_ps = [st.enter_context(nc.psum_tensor(f"ops{i}", [128, 512], F32))
                for i in range(2)]
        l_ps = [st.enter_context(nc.psum_tensor(f"lps{i}", [128, 512], F32))
                for i in range(2)]
        return (q_sb, k_sb, v_sb, e_sb, pbar, nsc, o_stg, s_ps, o_ps, l_ps)

    def p3(em, t):
        (q_sb, k_sb, v_sb, e_sb, pbar, nsc, o_stg, s_ps, o_ps, l_ps) = t
        em.step("SY", lambda s: s.dma_start(out=e_sb[:, :], in_=etab_d[:, :]),
                name="ld_e")
        vap = v_spill[:, :].rearrange("(kt p) o -> p kt o", p=128)
        vsv = v_sb[:, :].rearrange("p (kt o) -> p kt o", o=1024)
        for vc in range(4):
            em.step("GP" if vc % 2 == 0 else "SY",
                    lambda g, vc=vc: g.dma_start(
                        out=vsv[:, vc * 8:(vc + 1) * 8, :],
                        in_=vap[:, vc * 8:(vc + 1) * 8, :]),
                    name=f"a_v{vc}")
        for jj in range(2):
            em.step("SY", lambda s, jj=jj: s.dma_start(
                out=q_sb[jj][:, :],
                in_=qk_spill[jj * 128:(jj + 1) * 128, :]), name=f"a_q{jj}")
            em.step("SY", lambda s, jj=jj: s.dma_start(
                out=k_sb[jj][:, :],
                in_=qk_spill[(8 + jj) * 128:(9 + jj) * 128, :]),
                name=f"a_k{jj}")
        flat = []
        for j in range(NSLOT):
            for qc in range(8):
                items = sched[j][qc]
                for it, (kt, qlo, sl) in enumerate(items):
                    flat.append((j, qc, it, len(items), kt, qlo, sl))
        nflat = len(flat)

        def emit_st(g):
            j, qc, it, nk, kt, qlo, sl = flat[g]
            qhi = sl[-1][1]
            if qc == 0 and it == 0 and 2 <= j + 1 < NSLOT:
                jn = j + 1
                em.step("SY", lambda s, jn=jn: s.dma_start(
                    out=q_sb[jn % 2][:, :],
                    in_=qk_spill[jn * 128:(jn + 1) * 128, :]),
                    name=f"a_q{jn}", waits=(f"a_st_last{jn-2}",))
                em.step("SY", lambda s, jn=jn: s.dma_start(
                    out=k_sb[jn % 2][:, :],
                    in_=qk_spill[(8 + jn) * 128:(9 + jn) * 128, :]),
                    name=f"a_k{jn}", waits=(f"a_st_last{jn-2}",))
            def mmst(tt, j=j, qc=qc, kt=kt, qlo=qlo, qhi=qhi, g=g):
                return tt.matmul(
                    s_ps[g % NB][:, qlo:qhi],
                    k_sb[j % 2][:, kt * 128:(kt + 1) * 128],
                    q_sb[j % 2][:, qc * 512 + qlo:qc * 512 + qhi],
                    start=True, stop=True)
            em.step("PE", mmst, name=f"a_stG{g}",
                    waits=(f"a_q{j}", f"a_k{j}")
                    + ((f"a_expG{g-NB}",) if g >= NB else ()))
            if g == flat_last_of_slot[j]:
                em.mil[f"a_st_last{j}"] = em.mil[f"a_stG{g}"]
            diag = kt >= 4 * qc
            if diag:
                dq = 128 * (kt - 4 * qc)
                em.step("DVE", lambda v, g=g, dq=dq: v.tensor_tensor(
                    s_ps[g % NB][:, dq:dq + 128],
                    s_ps[g % NB][:, dq:dq + 128],
                    mask_sb[:, :], ALU.add),
                    name=f"a_maskG{g}", waits=(f"a_stG{g}",))
            def expf(a, j=j, sl=sl, g=g):
                last = None
                for (aa, bb, dk) in sl:
                    bi = bkeys[(j, dk)]
                    last = a.activation(
                        pbar[g % NB][:, aa:bb], s_ps[g % NB][:, aa:bb],
                        AF.Exp, bias=btab_sb[:, bi:bi + 1],
                        scale=SCALE)
                return last
            em.step("ACT", expf, name=f"a_expG{g}",
                    waits=((f"a_maskG{g}",) if diag else (f"a_stG{g}",))
                    + ("ld_consts",)
                    + ((f"a_avG{g-NB}",) if g >= NB else ()))

        def emit_av(g):
            j, qc, it, nk, kt, qlo, sl = flat[g]
            qhi = sl[-1][1]
            first = (it == 0)
            last_it = (it == nk - 1)
            def mmav(tt, j=j, qc=qc, kt=kt, qlo=qlo, qhi=qhi, g=g,
                     first=first, last_it=last_it):
                tt.matmul(o_ps[qc % 2][:, qlo:qhi],
                          v_sb[:, kt * 1024 + j * 128:
                               kt * 1024 + (j + 1) * 128],
                          pbar[g % NB][:, qlo:qhi],
                          start=first, stop=last_it)
                return tt.matmul(l_ps[qc % 2][:, qlo:qhi],
                                 ones_bf[:, :],
                                 pbar[g % NB][:, qlo:qhi],
                                 start=first, stop=last_it)
            em.step("PE", mmav, name=f"a_avG{g}",
                    waits=(f"a_expG{g}", f"a_v{kt//8}")
                    + ((f"a_norm_done{j}_{qc-2}",) if first and qc >= 2
                       else ())
                    + ((f"a_normprev{j}_{qc}",) if first and
                       f"a_normprev{j}_{qc}" in em.mil else ()))
            if not last_it:
                return
            war_le = ((f"a_rp{j}_{qc-2}",) if qc >= 2 else
                      ((f"a_rp{j-1}_{qc+6}",) if j >= 1 else ()))
            war_rp = ((f"a_on{j}_{qc-2}",) if qc >= 2 else
                      ((f"a_on{j-1}_{qc+6}",) if j >= 1 else ()))
            em.step("DVE", lambda v, j=j, qc=qc: v.tensor_tensor(
                nsc[(qc % 2) * 2][:, :], l_ps[qc % 2][:, :],
                e_sb[:, j * 512:(j + 1) * 512], ALU.mult),
                name=f"a_le{j}_{qc}",
                waits=(f"a_avG{g}", "ld_e") + war_le)
            em.step("ACT", lambda a, j=j, qc=qc: a.activation(
                nsc[(qc % 2) * 2][:, :], nsc[(qc % 2) * 2][:, :], AF.Ln),
                name=f"a_ln{j}_{qc}", waits=(f"a_le{j}_{qc}",))
            em.step("ACT", lambda a, j=j, qc=qc: a.activation(
                nsc[(qc % 2) * 2 + 1][:, :], nsc[(qc % 2) * 2][:, :],
                AF.Exp, scale=-1.0), name=f"a_rp{j}_{qc}",
                own_waits=(f"a_ln{j}_{qc}",), waits=war_rp)
            em.step("DVE", lambda v, j=j, qc=qc: v.tensor_tensor(
                nsc[(qc % 2) * 2 + 1][:, :], nsc[(qc % 2) * 2 + 1][:, :],
                e_sb[:, j * 512:(j + 1) * 512], ALU.mult),
                name=f"a_r{j}_{qc}", waits=(f"a_rp{j}_{qc}",))
            em.step("DVE", lambda v, j=j, qc=qc: v.tensor_tensor(
                o_stg[qc % 2][:, :], o_ps[qc % 2][:, :],
                nsc[(qc % 2) * 2 + 1][:, :], ALU.mult),
                name=f"a_on{j}_{qc}", own_waits=(f"a_r{j}_{qc}",),
                waits=((f"a_od{j}_{qc-2}",) if qc >= 2 else
                       ((f"a_od{j-1}_{qc+6}",) if j >= 1 else ())))
            em.mil[f"a_norm_done{j}_{qc}"] = em.mil[f"a_on{j}_{qc}"]
            if qc == 6:
                em.mil[f"a_normprev{j+1}_0"] = em.mil[f"a_norm_done{j}_6"]
            if qc == 7:
                em.mil[f"a_normprev{j+1}_1"] = em.mil[f"a_norm_done{j}_7"]
            em.step("SY", lambda s, j=j, qc=qc: s.dma_start(
                out=o_bounce[j * 128:(j + 1) * 128,
                             qc * 512:(qc + 1) * 512],
                in_=o_stg[qc % 2][:, :]), name=f"a_od{j}_{qc}",
                waits=(f"a_on{j}_{qc}",))
            if qc == 7:
                def ccj(g, sem, j=j):
                    g.wait_ge(em.sems["DS"], em.mil[f"a_od{j}_7"][1])
                    g.collective_compute(
                        "AllGather", ALU.bypass,
                        replica_groups=[[0, 1, 2, 3], [4, 5, 6, 7]],
                        ins=[o_bounce[j * 128:(j + 1) * 128, :].opt()],
                        outs=[o_gather[j * 512:(j + 1) * 512, :].opt()]
                    ).then_inc(sem, 1)
                em.raw("GP", ccj, name=f"cc2_{j}", inc=1, sem="CC2")

        flat_last_of_slot = {}
        for g, tile in enumerate(flat):
            flat_last_of_slot[tile[0]] = g
        for pos in range(nflat + DEPTH):
            if pos < nflat:
                emit_st(pos)
            if pos >= DEPTH:
                emit_av(pos - DEPTH)
        em.step("GP", lambda g: g.wait_ge(em.sems["DW"], em.c["DW"]) and None,
                inc=0)
        em.step("SY", lambda s: s.wait_ge(em.sems["DS"], em.c["DS"]) and None,
                inc=0)

    def p4_make(st):
        o_own = st.enter_context(nc.sbuf_tensor("oown", [128, NCH * TOK], BF16))
        w2_sb = [st.enter_context(nc.sbuf_tensor(f"w2_{i}", [128, 4096], BF16))
                 for i in range(3)]
        w2f = [st.enter_context(nc.sbuf_tensor(f"w2f{i}", [128, 3072], F32))
               for i in range(2)]
        t05 = [st.enter_context(nc.sbuf_tensor(f"p4t{i}", [128, TOK], F32))
               for i in range(2)]
        op_ps = [st.enter_context(nc.psum_tensor(f"opp{i}", [128, TOK], F32))
                 for i in range(2)]
        return (o_own, w2_sb, w2f, t05, op_ps)

    def p4(em, t):
        (o_own, w2_sb, w2f, t05, op_ps) = t
        em.step("SY", lambda s: s.dma_start(
            out=Rh[0][:, :].rearrange("p (c t) -> p c t", c=NCH),
            in_=r_spill[:, :].rearrange("(c p) t -> p c t", p=128)),
            name="p4_rld", waits=("r_spill",))
        def ldo(g, sem):
            pid = nc.partition_id([ET.Pool])
            reg = g.compute_val(pid)
            for i in g.Switch(reg, 8):
                rr = i % 4
                for jj in range(NSLOT):
                    g.wait_ge(em.sems["CC2"], jj + 1)
                    for q4 in range(4):
                        ic = jj * 4 + q4
                        g.dma_start(
                            out=o_own[:, ic * TOK:(ic + 1) * TOK],
                            in_=o_gather[ic * 128:(ic + 1) * 128,
                                         rr * TOK:(rr + 1) * TOK]
                        ).then_inc(sem, 16)
        def emit_wa(oc):
            em.step("GP", lambda g, oc=oc: g.dma_start(
                out=w2_sb[oc % 3][:, 0:1024],
                in_=w2_d[oc * 128:(oc + 1) * 128, 0:1024]), name=f"p4_wa{oc}",
                waits=(f"p4_mm{oc-3}",) if oc >= 3 else ())

        def emit_wf(oc):
            em.step("SY", lambda s, oc=oc: s.dma_start(
                out=w2f[oc % 2][:, :],
                in_=w2_d[oc * 128:(oc + 1) * 128, 1024:4096]),
                name=f"p4_wf{oc}",
                waits=(f"p4_cv{oc-2}",) if oc >= 2 else ())

        def emit_cv(oc):
            em.step("DVE", lambda v, oc=oc: v.tensor_copy(
                w2_sb[oc % 3][:, 1024:4096], w2f[oc % 2][:, :]),
                name=f"p4_cv{oc}",
                waits=(f"p4_wf{oc}",)
                + ((f"p4_mm{oc-3}",) if oc >= 3 else ()))

        emit_wa(0)
        emit_wa(1)
        em.raw("GP", ldo, name="p4_oo", inc=16 * NCH, sem="DW")
        emit_wf(0)
        emit_wf(1)
        emit_cv(0)
        for oc in range(NCH):
            if oc + 2 < NCH:
                emit_wa(oc + 2)
                emit_wf(oc + 2)
            if oc + 1 < NCH:
                emit_cv(oc + 1)
            def mmo(tt, oc=oc):
                last = None
                for ic in range(NCH):
                    for h in range(2):
                        last = tt.matmul(
                            op_ps[oc % 2][:, h * 512:(h + 1) * 512],
                            w2_sb[oc % 3][:, ic * 128:(ic + 1) * 128],
                            o_own[:, ic * TOK + h * 512:ic * TOK + (h + 1) * 512],
                            start=(ic == 0), stop=(ic == NCH - 1))
                return last
            em.step("PE", mmo, name=f"p4_mm{oc}",
                    waits=("p4_oo", f"p4_wa{oc}", f"p4_cv{oc}")
                    + ((f"p4_t{oc-2}",) if oc >= 2 else ()))
            em.step("ACT", lambda a, oc=oc: a.activation(
                t05[oc % 2][:, :], op_ps[oc % 2][:, :], AF.Identity,
                bias=bout_sb[:, oc:oc + 1]), name=f"p4_t{oc}",
                waits=(f"p4_mm{oc}",)
                + ((f"p4_acc{oc-2}",) if oc >= 2 else ()))
            em.step("DVE", lambda v, oc=oc: v.tensor_tensor(
                Rh[0][:, oc * TOK:(oc + 1) * TOK], Rh[0][:, oc * TOK:(oc + 1) * TOK],
                t05[oc % 2][:, :], ALU.add), name=f"p4_acc{oc}",
                waits=(f"p4_t{oc}", "p4_rld"))
        em.step("GP", lambda g: g.wait_ge(em.sems["DW"], em.c["DW"]) and None,
                inc=0)
        em.step("SY", lambda s: s.wait_ge(em.sems["DS"], em.c["DS"]) and None,
                inc=0)

    def pdump_spill(em):
        em.step("GP", lambda g: g.dma_start(out=y_d[:, :], in_=r_spill[:, :]))
        em.step("GP", lambda g: g.wait_ge(em.sems["DW"], em.c["DW"]) and None,
                inc=0)

    def pdump(em):
        for c in range(NCH):
            em.step("GP", lambda g, c=c: g.dma_start(
                out=y_d[c * 128:(c + 1) * 128, :],
                in_=Rh[0][:, c * TOK:(c + 1) * TOK]), name=f"dump{c}")
        if _DUMPQKV:
            em.step("SY", lambda s: s.dma_start(out=dbgq_d[:, :],
                                                in_=qk_spill[:, :]))
            em.step("SY", lambda s: s.dma_start(out=dbgv_d[:, :],
                                                in_=v_spill[:, :]))
        if _DUMPO:
            em.step("SY", lambda s: s.dma_start(out=dbgo_d[:, :],
                                                in_=o_bounce[:, :]))
        if _DUMPQKV or _DUMPO:
            em.step("SY", lambda s: s.wait_ge(em.sems["DS"], em.c["DS"])
                    and None, inc=0)
        em.step("GP", lambda g: g.wait_ge(em.sems["DW"], em.c["DW"]) and None,
                inc=0)

    with stack:
        with ExitStack() as stA:
            Rh[0] = stA.enter_context(
                nc.sbuf_tensor("R", [128, NCH * TOK], BF16))
            run_phase("p0", p0)
            if _PHASES >= 1:
                with ExitStack() as st:
                    t1 = make_ffn_tensors(st, "f1", True)
                    run_phase("p1",
                              lambda em: ffn_phase(em, "f1", t1, final=False))
            if _PHASES >= 2:
                with ExitStack() as st:
                    t2 = p2a_make(st)
                    run_phase("p2a", lambda em: p2a(em, t2))
            if _PHASES <= 1:
                run_phase("pdump", pdump)
        Rh[0] = None
        if _PHASES >= 2:
            with ExitStack() as st:
                t2 = p2b_make(st)
                run_phase("p2b", lambda em: p2b(em, t2))
        if _PHASES >= 3:
            with ExitStack() as st:
                t3 = p3_make(st)
                run_phase("p3", lambda em: p3(em, t3))
        with ExitStack() as stB:
            if _PHASES >= 4:
                Rh[0] = stB.enter_context(
                    nc.sbuf_tensor("R2", [128, NCH * TOK], BF16))
                with ExitStack() as st:
                    t4 = p4_make(st)
                    run_phase("p4", lambda em: p4(em, t4))
            if _PHASES >= 5:
                with ExitStack() as st:
                    t5 = make_ffn_tensors(st, "f5", False)
                    run_phase("p5",
                              lambda em: ffn_phase(em, "f5", t5, final=True))
            elif _PHASES >= 4:
                run_phase("pdump", pdump)
            elif _PHASES >= 2:
                run_phase("pdump_s", pdump_spill)
    return nc



def _stage_inputs(x, wqkv, bqkv, wout, bout, up, down, perm):
    sched = attn_schedule()
    bkeys = bias_keys(sched)
    nbk = len(bkeys)
    slopes = _slopes()
    wqkv_p = wqkv[:, perm]
    w2t = np.ascontiguousarray(wout[perm, :].T)
    bout_p = bout[perm]

    def tileK(a, kchunks, width):
        return np.ascontiguousarray(
            a.reshape(kchunks, 128, width).transpose(1, 0, 2)
            .reshape(128, kchunks * width))

    up_img = np.concatenate([tileK(up[h], 2, 1024)[None] for h in range(FH)],
                            axis=0).reshape(FH * 128, 2048)
    dn_img = np.concatenate([tileK(down[h], 8, 256)[None] for h in range(FH)],
                            axis=0).reshape(FH * 128, 2048)
    mask = np.where(np.arange(128)[:, None] <= np.arange(128)[None, :],
                    0.0, NEG).astype(np.float32)
    onesc = np.ones((128, 128), np.float32)
    ones1 = np.ones((1, 128), np.float32)

    ins = []
    for c in range(8):
        b, r = c // 4, c % 4
        heads = [4 * jj + r for jj in range(NSLOT)]
        x_in = np.ascontiguousarray(x[b, TOK * r:TOK * (r + 1), :][:, perm].T)
        qk_blocks = []
        for jj in range(NSLOT):
            h = heads[jj]
            A = wqkv_p[128 * h:128 * h + 128, :]
            qk_blocks.append(tileK(np.ascontiguousarray(A.T), 32, 128))
        for jj in range(NSLOT):
            h = heads[jj]
            A = wqkv_p[4096 + 128 * h:4096 + 128 * h + 128, :]
            qk_blocks.append(tileK(np.ascontiguousarray(A.T), 32, 128))
        w1qk = np.concatenate([bb[None] for bb in qk_blocks], axis=0)\
            .reshape(16 * 128, 4096)
        W1v = np.ascontiguousarray(np.concatenate(
            [wqkv_p[8192 + 128 * h:8192 + 128 * h + 128, :] for h in heads],
            axis=0).T)
        w1v = np.concatenate(
            [tileK(np.ascontiguousarray(W1v[:, 256 * vg:256 * (vg + 1)]),
                   32, 256)[None] for vg in range(4)],
            axis=0).reshape(4 * 128, 8192)
        w2_ord = np.empty_like(w2t)
        for ic in range(NCH):
            h = 4 * (ic // 4) + ic % 4
            w2_ord[128 * ic:128 * (ic + 1), :] = w2t[128 * h:128 * (h + 1), :]
        w2_img = np.concatenate(
            [tileK(np.ascontiguousarray(w2_ord[:, 128 * oc:128 * (oc + 1)]),
                   32, 128)[None] for oc in range(NCH)],
            axis=0).reshape(32 * 128, 4096)
        bqk = np.empty((128, 16), np.float32)
        for jj in range(NSLOT):
            bqk[:, jj] = bqkv[128 * heads[jj]:128 * heads[jj] + 128]
            bqk[:, 8 + jj] = bqkv[4096 + 128 * heads[jj]:
                                  4096 + 128 * heads[jj] + 128]
        bv_ord = np.concatenate(
            [bqkv[8192 + 128 * h:8192 + 128 * h + 128] for h in heads])
        bvb = np.ascontiguousarray(
            np.broadcast_to(bv_ord[None, :], (128, 1024))).astype(np.float32)
        bout_img = np.ascontiguousarray(
            bout_p.reshape(32, 128).T).astype(np.float32)
        btab = np.zeros((128, nbk), np.float32)
        for (jj, dk), idx in bkeys.items():
            btab[:, idx] = (slopes[heads[jj]]
                            * (np.arange(128) + dk)).astype(np.float32)
        etab = np.zeros((128, 8 * 512), np.float32)
        qq = np.arange(512)
        for jj in range(NSLOT):
            W = W_SLOT[jj]
            qref = (qq // W + 1) * W - 1
            ev = np.exp(slopes[heads[jj]] * (qref - qq)).astype(np.float32)
            etab[:, jj * 512:(jj + 1) * 512] = ev[None, :]
        ins.append({
            "x_in": x_in, "w1qk": w1qk, "w1v": w1v, "w2": w2_img,
            "up_w": up_img, "dn_w": dn_img, "bqk": bqk, "bvb": bvb,
            "bout": bout_img, "btab": btab, "etab": etab, "maskt": mask,
            "onesc": onesc, "onesr": ones1,
            "onecol": np.ones((128, 1), np.float32),
        })
    return ins, nbk


def kernel(L0_attn_x, L0_attn_w_qkv, L0_attn_b_qkv, L0_attn_w_out,
           L0_attn_b_out, L0_ffn_up, L0_ffn_down, L0_ffn_perm):
    x = np.asarray(L0_attn_x, np.float32)
    wqkv = np.asarray(L0_attn_w_qkv, np.float32)
    bqkv = np.asarray(L0_attn_b_qkv, np.float32)
    wout = np.asarray(L0_attn_w_out, np.float32)
    bout = np.asarray(L0_attn_b_out, np.float32)
    up = np.asarray(L0_ffn_up, np.float32)
    down = np.asarray(L0_ffn_down, np.float32)
    perm = np.asarray(L0_ffn_perm).astype(np.int64)
    unperm = np.argsort(perm)

    ins, nbk = _stage_inputs(x, wqkv, bqkv, wout, bout, up, down, perm)
    nc = build_program(nbk)
    res = run_bass_kernel_spmd(nc, ins, core_ids=list(range(8)))

    out = np.empty((B, S, D), np.float32)
    for c in range(8):
        b, r = c // 4, c % 4
        y = np.asarray(res.results[c]["y"], np.float32)
        out[b, TOK * r:TOK * (r + 1), :] = y.T[:, unperm]
    return out
